# version 11 (final)
# baseline (speedup 1.0000x reference)
import numpy as np
import ml_dtypes
from contextlib import ExitStack

import concourse.bass as bass
import concourse.mybir as mybir
from concourse.bass_utils import run_bass_kernel_spmd

F32 = mybir.dt.float32
BF16 = mybir.dt.bfloat16
AF = mybir.ActivationFunctionType
ALU = mybir.AluOpType
AX = mybir.AxisListType.X

B, S, D = 4, 4096, 1024
SL = 2048
NCH = 8
IF = 2816
NIT = IF // 128
WSL = 384
BUS = 128
SLOTS = 64
CMAG = 12582912.0
EPS_RMS = 1.1920929e-07
EPS_LN = 1e-5
SC_ATTN = float(1.0 / np.sqrt(32.0))
SC_LM = float(1.0 / 32.0)

PAIRS = [[0, 1], [2, 3], [4, 5], [6, 7]]
ALL8 = [list(range(8))]


class SemPlan:
    def __init__(self, nc, stack, names):
        self._h = {}
        self.c = {}
        for n in names:
            self._h[n] = stack.enter_context(nc.semaphore(n))
            self.c[n] = 0

    def h(self, n):
        return self._h[n]

    def ti(self, instr, n, k=1):
        self.c[n] += k
        instr.then_inc(self._h[n], k)
        return self.c[n]

    def w(self, E, n, v=None):
        E.wait_ge(self._h[n], self.c[n] if v is None else v)


def build_nc(taps=(), upto=None):
    nc = bass.Bass("TRN2", target_bir_lowering=False)
    dt = nc.dram_tensor

    x_in = dt("x", [SL, D], F32, kind="ExternalInput").ap()
    c_row_in = dt("c_row", [1, D], F32, kind="ExternalInput").ap()
    wmod_sl = dt("wmod_sl", [D, 3072], F32, kind="ExternalInput").ap()
    wn_in = dt("wn_in", [8, 128], F32, kind="ExternalInput").ap()
    wn_out = dt("wn_out", [8, 128], F32, kind="ExternalInput").ap()
    rw1_in = dt("rw1", [D, 512], F32, kind="ExternalInput").ap()
    rw2_in = dt("rw2", [512, 3], F32, kind="ExternalInput").ap()
    rb2_in = dt("rb2", [1, 3], F32, kind="ExternalInput").ap()
    lam_in = dt("lam_in", [128, 8], F32, kind="ExternalInput").ap()
    wqk_in = dt("wqk", [D, 128], F32, kind="ExternalInput").ap()
    wv_in = dt("wv_in", [D, 32], F32, kind="ExternalInput").ap()
    wo_in = dt("wo_in", [32, D], F32, kind="ExternalInput").ap()
    slam_in = dt("slam_in", [128, 1], F32, kind="ExternalInput").ap()
    lm_in = dt("lm_in", [SLOTS, D], F32, kind="ExternalInput").ap()
    mixw_in = dt("mixw_in", [2 * D, D], F32, kind="ExternalInput").ap()
    busc_in = dt("busc_in", [D, BUS], F32, kind="ExternalInput").ap()
    buse_in = dt("buse_in", [BUS, D], F32, kind="ExternalInput").ap()
    busp_in = dt("busp_in", [BUS, BUS], F32, kind="ExternalInput").ap()
    clipx_in = dt("clipx_in", [1, 1], F32, kind="ExternalInput").ap()
    clipd_in = dt("clipd_in", [1, 1], F32, kind="ExternalInput").ap()
    wg_sl = dt("wg_sl", [WSL, D], F32, kind="ExternalInput").ap()
    wu_sl = dt("wu_sl", [WSL, D], F32, kind="ExternalInput").ap()
    wd_sl = dt("wd_sl", [128, IF], F32, kind="ExternalInput").ap()
    flag_in = dt("flag1", [128, 1], F32, kind="ExternalInput").ap()
    ident32_in = dt("ident32_in", [128, 128], F32, kind="ExternalInput").ap()
    ones32_in = dt("ones32_in", [128, 128], F32, kind="ExternalInput").ap()
    grp4_in = dt("grp4_in", [128, 4], F32, kind="ExternalInput").ap()

    y_out = dt("y", [SL, D], F32, kind="ExternalOutput").ap()

    xbf_scr = dt("xbf_scr", [SL, D], BF16).ap()
    x2_scr = dt("x2_scr", [NCH * 128, SL], BF16).ap()
    hpre_scr = dt("hpre_scr", [NIT * 128, SL], BF16).ap()
    lm_scr = dt("lm_scr", [1, SLOTS], F32).ap()

    ag_mod_i = dt("ag_mod_i", [128, 24], F32).ap()
    ag_mod_o = dt("ag_mod_o", [256, 24], F32).ap()
    ag_wg_i = dt("ag_wg_i", [WSL, D], BF16).ap()
    ag_wg_o = dt("ag_wg_o", [8 * WSL, D], BF16).ap()
    ag_wu_i = dt("ag_wu_i", [WSL, D], BF16).ap()
    ag_wu_o = dt("ag_wu_o", [8 * WSL, D], BF16).ap()
    ag_wd_i = dt("ag_wd_i", [128, IF], BF16).ap()
    ag_wd_o = dt("ag_wd_o", [D, IF], BF16).ap()
    ag_sc_i = dt("ag_sc_i", [896, 1], F32).ap()
    ag_sc_o = dt("ag_sc_o", [8 * 896, 1], F32).ap()
    ag_re_i = dt("ag_re_i", [256, 8], F32).ap()
    ag_re_o = dt("ag_re_o", [512, 8], F32).ap()
    ag_kv_i = dt("ag_kv_i", [96, SL], BF16).ap()
    ag_kv_o = dt("ag_kv_o", [192, SL], BF16).ap()
    ag_bus_i = dt("ag_bus_i", [128, 8], F32).ap()
    ag_bus_o = dt("ag_bus_o", [256, 8], F32).ap()
    ar_h2_i = dt("ar_h2_i", [1, 1], F32).ap()
    ar_h2_o = dt("ar_h2_o", [1, 1], F32).ap()
    ar_hp_i = dt("ar_hp_i", [1, 1], F32).ap()
    ar_hp_o = dt("ar_hp_o", [1, 1], F32).ap()

    tap_drams = {n: dt("tap_" + n, [p, f], F32, kind="ExternalOutput").ap()
                 for n, p, f in taps}

    st = ExitStack()
    with st:
        e = st.enter_context
        SM = SemPlan(nc, st, ["dL", "dG", "tE", "vE", "sE", "gE", "cC"])
        blk = e(nc.Block())

        def gate(E):
            for n in ("tE", "vE", "sE", "dL", "dG", "cC"):
                SM.w(E, n)

        def vg(E):
            E.tensor_single_scalar(out=vjnk[:, 0:8], in_=vjnk[:, 8:16],
                                   scalar=1.0, op=ALU.mult)

        def tapout(E, name):
            if name in tap_drams:
                return tap_drams[name]
            return None

        def tapdump(name, src_ap, wname=None, wval=None):
            if name not in tap_drams:
                return
            @blk.gpsimd
            def _(E):
                if wname is not None:
                    SM.w(E, wname, wval)
                SM.ti(E.dma_start(out=tap_drams[name], in_=src_ap), "dG", 16)

        def finish_dummy():
            @blk.sync
            def _(E):
                gate(E)
                for st_ in range(16):
                    SM.ti(E.dma_start(
                        out=bass.AP(y_out.tensor, st_ * 128 * D,
                                    [[D, 128], [1, D]]),
                        in_=fsc0[:, 0:1024]), "dL", 16)
                gate(E)
            for eng in (blk.tensor, blk.vector, blk.scalar, blk.gpsimd):
                @eng
                def _(E):
                    gate(E)

        ident32 = e(nc.sbuf_tensor("ident32", [128, 128], F32))
        identbf = e(nc.sbuf_tensor("identbf", [128, 128], BF16))
        ones32 = e(nc.sbuf_tensor("ones32", [128, 128], F32))
        onesbf = e(nc.sbuf_tensor("onesbf", [128, 128], BF16))
        grp4bf = e(nc.sbuf_tensor("grp4bf", [128, 4], BF16))
        rowA = e(nc.sbuf_tensor("rowA", [64, 2048], F32))
        rowB = e(nc.sbuf_tensor("rowB", [64, 2048], F32))
        tiny = e(nc.sbuf_tensor("tiny", [128, 80], F32))
        tinyb = e(nc.sbuf_tensor("tinyb", [128, 16], BF16))
        vjnk = e(nc.sbuf_tensor("vjnk", [128, 16], F32))
        modA = e(nc.sbuf_tensor("modA", [128, 24], F32))
        modB = e(nc.sbuf_tensor("modB", [128, 24], F32))
        onepca = e(nc.sbuf_tensor("onepca", [128, 8], F32))
        onepcb = e(nc.sbuf_tensor("onepcb", [128, 8], F32))
        wni_pp = e(nc.sbuf_tensor("wni_pp", [128, 8], F32))
        wno_pp = e(nc.sbuf_tensor("wno_pp", [128, 8], F32))
        lam_pp = e(nc.sbuf_tensor("lam_pp", [128, 8], F32))
        omlam_pp = e(nc.sbuf_tensor("omlam_pp", [128, 8], F32))
        silucT = e(nc.sbuf_tensor("silucT", [128, 8], BF16))
        wqk_sb = e(nc.sbuf_tensor("wqk_sb", [128, 8 * 128], BF16))
        wv_sb = e(nc.sbuf_tensor("wv_sb", [128, 8 * 32], BF16))
        wo_sb = e(nc.sbuf_tensor("wo_sb", [32, D], BF16))
        lm_nat = e(nc.sbuf_tensor("lm_nat", [SLOTS, D], BF16))
        lmT = e(nc.sbuf_tensor("lmT", [128, 8 * SLOTS], BF16))
        busC = e(nc.sbuf_tensor("busC", [128, 8 * 128], BF16))
        busP = e(nc.sbuf_tensor("busP", [128, 128], BF16))
        busE = e(nc.sbuf_tensor("busE", [128, D], BF16))
        rw1_sb = e(nc.sbuf_tensor("rw1_sb", [128, 8 * 512], BF16))
        rw2_sb = e(nc.sbuf_tensor("rw2_sb", [128, 4 * 3], BF16))
        rb2_sb = e(nc.sbuf_tensor("rb2_sb", [1, 3], F32))
        flag_pp = e(nc.sbuf_tensor("flag_pp", [128, 1], F32))
        slam_pp = e(nc.sbuf_tensor("slam_pp", [128, 1], F32))
        routeacc = e(nc.sbuf_tensor("routeacc", [128, 8], F32))
        emafin = e(nc.sbuf_tensor("emafin", [128, 8], F32))
        carry_sb = e(nc.sbuf_tensor("carry_sb", [128, 8], F32))
        busacc = e(nc.sbuf_tensor("busacc", [128, 8], F32))
        scales_all = e(nc.sbuf_tensor("scales_all", [128, 72], F32))
        gsc = e(nc.sbuf_tensor("gsc", [128, 24], F32))
        usc = e(nc.sbuf_tensor("usc", [128, 24], F32))
        dsc = e(nc.sbuf_tensor("dsc", [128, 8], F32))
        dscf = e(nc.sbuf_tensor("dscf", [128, 8], F32))
        fsc0 = e(nc.sbuf_tensor("fsc0", [128, SL], F32))
        fsc1 = e(nc.sbuf_tensor("fsc1", [128, SL], F32))
        bsc0 = e(nc.sbuf_tensor("bsc0", [128, SL], BF16))
        bsc1 = e(nc.sbuf_tensor("bsc1", [128, SL], BF16))

        T_ZERO, T_PPM, T_PF, T_NC = 0, 1, 2, 3
        T_SGX, T_SGD, T_ISX, T_ISH, T_SX = 5, 6, 7, 8, 9
        T_SCR = 16
        T_BUSV = 48

        ROW_ALPHA = lambda: rowA[0:1, :]
        ROW_BETA = lambda: rowA[32:33, :]
        ROW_MEAN = lambda: rowB[0:1, :]
        ROW_TMP = lambda: rowB[32:33, :]
        X_CLX, X_CLD = 73, 74
        X_SGX, X_SGD = 56, 57
        X_H2L, X_H2G, X_HPL, X_HPG = 58, 59, 60, 61
        X_NPM, X_PF = 62, 63
        X_SX, X_ISXv, X_SH, X_ISHv = 64, 65, 66, 67

        def ch(buf, j, a=0, b=SL):
            return buf[:, j * SL + a: j * SL + b]

        @blk.sync
        def _(E):
            for _o, _i in ((ident32[:], ident32_in), (ones32[:], ones32_in),
                           (rb2_sb[:], rb2_in), (flag_pp[:], flag_in),
                           (slam_pp[:], slam_in), (lam_pp[:], lam_in),
                           (rowB[0:8, 0:128], wn_in), (rowB[32:40, 0:128], wn_out),
                           (rowA[0:1, 0:D], c_row_in),
                           (tiny[0:1, X_CLX:X_CLX + 1], clipx_in),
                           (tiny[0:1, X_CLD:X_CLD + 1], clipd_in)):
                SM.ti(E.dma_start(out=_o, in_=_i), "dL", 16)
        ld_small = SM.c["dL"]

        @blk.gpsimd
        def _(E):
            for _o, _i in (
                    (identbf[:], ident32_in), (onesbf[:], ones32_in),
                    (grp4bf[:], grp4_in),
                    (wqk_sb[:], bass.AP(wqk_in.tensor, 0,
                                        [[128, 128], [128 * 128, 8], [1, 128]])),
                    (wv_sb[:], bass.AP(wv_in.tensor, 0,
                                       [[32, 128], [128 * 32, 8], [1, 32]])),
                    (wo_sb[:, :], wo_in), (lm_nat[:, :], lm_in),
                    (busC[:], bass.AP(busc_in.tensor, 0,
                                      [[128, 128], [128 * 128, 8], [1, 128]])),
                    (busP[:], busp_in), (busE[:], buse_in),
                    (rw1_sb[:], bass.AP(rw1_in.tensor, 0,
                                        [[512, 128], [128 * 512, 8], [1, 512]])),
                    (rw2_sb[:], bass.AP(rw2_in.tensor, 0,
                                        [[3, 128], [128 * 3, 4], [1, 3]]))):
                SM.ti(E.dma_start(out=_o, in_=_i), "dG", 16)
            SM.ti(E.dma_start(out=xbf_scr, in_=x_in), "dG", 16)
        ld_bf = SM.c["dG"] - 16
        ld_xcast = SM.c["dG"]
        if upto == "loads":
            finish_dummy()
            return nc

        with ExitStack() as p0:
            e0 = p0.enter_context
            ps0 = e0(nc.psum_tensor("ps0", [128, 512], F32))
            ps1 = e0(nc.psum_tensor("ps1", [128, 1024], F32))
            psb = e0(nc.psum_tensor("psb", [128, 1024], BF16))
            wmb = e0(nc.sbuf_tensor("wmb", [128, 3072], BF16))
            wps = e0(nc.sbuf_tensor("wps", [128, IF], F32))
            wpu = e0(nc.sbuf_tensor("wpu", [128, IF], F32))
            wqo = e0(nc.sbuf_tensor("wqo", [128, IF], BF16))
            modev = e0(nc.sbuf_tensor("modev", [128, 24], F32))

            @blk.scalar
            def _(E):
                SM.w(E, "dL", ld_small)
                E.activation(out=tiny[0:1, X_SGX:X_SGX + 1],
                             in_=tiny[0:1, X_CLX:X_CLX + 1], func=AF.Sigmoid)
                E.activation(out=tiny[0:1, X_SGD:X_SGD + 1],
                             in_=tiny[0:1, X_CLD:X_CLD + 1], func=AF.Sigmoid)
                SM.ti(E.activation(out=rowA[0:1, 1024:2048],
                                   in_=rowA[0:1, 0:1024], func=AF.Silu),
                      "sE", 1)
            se_siluc = SM.c["sE"]
            if upto == "sigm_a":
                finish_dummy()
                return nc

            @blk.vector
            def _(E):
                E.memset(tiny[:, T_ZERO:T_ZERO + 1], 0.0)
                E.memset(tiny[:, 68:69], CMAG)
                E.memset(tiny[:, 69:70], EPS_LN)
                E.memset(tiny[:, 70:71], EPS_RMS)
                E.memset(tiny[:, 71:72], SC_ATTN)
                E.memset(tiny[:, 72:73], SC_LM)
                SM.w(E, "dL", ld_small)
                E.tensor_scalar(out=omlam_pp[:], in0=lam_pp[:], scalar1=1.0,
                                scalar2=-1.0, op0=ALU.subtract, op1=ALU.mult)
                SM.w(E, "sE", se_siluc)
                SM.ti(E.tensor_copy(out=bsc0[0:1, 0:D],
                                    in_=rowA[0:1, 1024:2048]), "vE", 1)
            ve_cbf = SM.c["vE"]
            if upto == "sigm_b":
                finish_dummy()
                return nc

            @blk.tensor
            def _(E):
                SM.w(E, "dL", ld_small)
                _last = E.transpose(ps0[:, 0:8], rowB[0:8, 0:128],
                            ident32[0:8, 0:8])
                E.nop()
                _last = E.transpose(ps0[:, 8:16], rowB[32:40, 0:128],
                            ident32[32:40, 32:40])
                E.nop()
                SM.w(E, "sE", se_siluc)
                _last = E.matmul(ps0[:, 16:17], ones32[0:1, 0:128],
                         tiny[0:1, X_SGX:X_SGX + 1], start=True, stop=True)
                _last = E.matmul(ps0[:, 17:18], ones32[0:1, 0:128],
                         tiny[0:1, X_SGD:X_SGD + 1], start=True, stop=True)
                SM.w(E, "vE", ve_cbf)
                SM.w(E, "dG", ld_bf)
                for j in range(NCH):
                    _last = E.transpose(psb[:, 32 + 2 * j:33 + 2 * j],
                                bsc0[0:1, j * 128:(j + 1) * 128],
                                identbf[0:1, 0:1])
                    E.nop()
                for j in range(NCH):
                    _last = E.transpose(psb[:, 64 + j * 64:64 + (j + 1) * 64],
                                lm_nat[:, j * 128:(j + 1) * 128],
                                identbf[0:SLOTS, 0:SLOTS])
                    E.nop()
                SM.ti(_last, "tE", 1)
            te_tr0 = SM.c["tE"]
            if upto == "sigm_c":
                finish_dummy()
                return nc

            @blk.vector
            def _(E):
                SM.w(E, "tE", te_tr0)
                import os as _os
                _skip = _os.environ.get("KSKIP", "")
                if "lmT" not in _skip:
                    SM.ti(E.tensor_copy(out=lmT[:, 0:8 * SLOTS],
                                        in_=psb[:, 64:576]), "vE", 1)
                else:
                    SM.ti(E.engine_nop(), "vE", 1)
                if "wn" not in _skip:
                    E.tensor_copy(out=wni_pp[:], in_=ps0[:, 0:8])
                    E.tensor_copy(out=wno_pp[:], in_=ps0[:, 8:16])
                if "sil" not in _skip:
                    E.tensor_copy(out=silucT[:],
                                  in_=bass.AP(psb, 32, [[1024, 128], [2, 8]]))
                if "sg" not in _skip:
                    E.tensor_copy(out=tiny[:, T_SGX:T_SGX + 1], in_=ps0[:, 16:17])
                SM.ti(E.tensor_copy(out=tiny[:, T_SGD:T_SGD + 1],
                                    in_=ps0[:, 17:18]), "vE", 1)
            ve_evac0 = SM.c["vE"]
            if upto == "sigm":
                finish_dummy()
                return nc

            dg_wm0 = SM.c["dG"]

            @blk.gpsimd
            def _(E):
                for j in range(NCH):
                    if j > 0:
                        SM.w(E, "tE", te_tr0 + j)
                    SM.ti(E.dma_start(out=wmb[:], in_=bass.AP(
                        wmod_sl.tensor, j * 128 * 3072,
                        [[3072, 128], [1, 3072]])), "dG", 16)

            @blk.tensor
            def _(E):
                SM.w(E, "vE", ve_evac0)
                for j in range(NCH):
                    SM.w(E, "dG", dg_wm0 + 16 * (j + 1))
                    if j > 1:
                        SM.w(E, "vE", ve_evac0 + (j - 1))
                    for t in range(24):
                        ins = E.matmul(
                            ps1[:, (j % 2) * 512 + t:(j % 2) * 512 + t + 1],
                            wmb[:, t * 128:(t + 1) * 128],
                            silucT[:, j:j + 1], start=True, stop=True)
                    SM.ti(ins, "tE", 1)
            te_mod = SM.c["tE"]

            @blk.vector
            def _(E):
                E.memset(modev[:], 0.0)
                for j in range(NCH):
                    SM.w(E, "tE", te_tr0 + (j + 1))
                    SM.ti(E.tensor_tensor(
                        out=modev[:], in0=modev[:],
                        in1=ps1[:, (j % 2) * 512:(j % 2) * 512 + 24],
                        op=ALU.add), "vE", 1)
                    vg(E)
            ve_mod = SM.c["vE"]

            @blk.sync
            def _(E):
                SM.w(E, "vE", ve_mod)
                SM.ti(E.dma_start(out=ag_mod_i, in_=modev[:]), "dL", 16)
            dl_modout = SM.c["dL"]
            if upto == "mod":
                finish_dummy()
                return nc

            wtiles = [(wg_sl, t * 128, D, ag_wg_i, t * 128, t * 128) for t in range(3)] \
                + [(wu_sl, t * 128, D, ag_wu_i, t * 128, 384 + t * 128) for t in range(3)] \
                + [(wd_sl, 0, IF, ag_wd_i, 0, 768)]
            dl_wp0 = SM.c["dL"]
            ve_wp0 = SM.c["vE"]
            se_wp0 = SM.c["sE"]

            C_INV, C_SC = T_SCR + 24, T_SCR + 32

            @blk.sync
            def _(E):
                for i, (src, roff, cols, agdst, agro, sco) in enumerate(wtiles):
                    if i > 0:
                        SM.w(E, "sE", se_wp0 + i)
                        SM.w(E, "vE", ve_wp0 + 2 * i)
                        ps_, pr, pc, pagd, pagro, psco = wtiles[i - 1]
                        SM.ti(E.dma_start(out=bass.AP(pagd.tensor, pagro * pc,
                                                      [[pc, 128], [1, pc]]),
                                          in_=wqo[:, 0:pc]), "dL", 16)
                        SM.ti(E.dma_start(
                            out=bass.AP(ag_sc_i.tensor, psco, [[1, 128], [1, 1]]),
                            in_=tiny[:, C_SC + i - 1:C_SC + i],
                            allow_slow_non_contiguous=True), "dL", 16)
                    SM.ti(E.dma_start(out=wps[:, 0:cols], in_=bass.AP(
                        src.tensor, roff * cols, [[cols, 128], [1, cols]])), "dL", 16)
                SM.w(E, "vE", ve_wp0 + 2 * len(wtiles))
                ps_, pr, pc, pagd, pagro, psco = wtiles[-1]
                SM.ti(E.dma_start(out=bass.AP(pagd.tensor, pagro * pc,
                                              [[pc, 128], [1, pc]]),
                                  in_=wqo[:, 0:pc]), "dL", 16)
                SM.ti(E.dma_start(
                    out=bass.AP(ag_sc_i.tensor, psco, [[1, 128], [1, 1]]),
                    in_=tiny[:, C_SC + 6:C_SC + 7],
                    allow_slow_non_contiguous=True), "dL", 16)
            dl_wpend = SM.c["dL"]

            @blk.vector
            def _(E):
                for i, (src, roff, cols, agdst, agro, sco) in enumerate(wtiles):
                    SM.w(E, "dL", dl_wp0 + 48 * i + 16)
                    E.reduce_max(out=tiny[:, T_SCR:T_SCR + 1], in_=wps[:, 0:cols],
                                 axis=AX, apply_absolute_value=True)
                    vg(E)
                    E.tensor_scalar(out=tiny[:, T_SCR:T_SCR + 1],
                                    in0=tiny[:, T_SCR:T_SCR + 1],
                                    scalar1=1e-30, scalar2=None, op0=ALU.max)
                    vg(E)
                    E.reciprocal(out=tiny[:, C_INV + i:C_INV + i + 1],
                                 in_=tiny[:, T_SCR:T_SCR + 1])
                    vg(E)
                    E.tensor_scalar(out=tiny[:, C_INV + i:C_INV + i + 1],
                                    in0=tiny[:, C_INV + i:C_INV + i + 1],
                                    scalar1=127.0, scalar2=None, op0=ALU.mult)
                    SM.ti(E.tensor_scalar(out=tiny[:, C_SC + i:C_SC + i + 1],
                                          in0=tiny[:, T_SCR:T_SCR + 1],
                                          scalar1=1.0 / 127.0, scalar2=None,
                                          op0=ALU.mult), "vE", 1)
                    SM.w(E, "sE", se_wp0 + i + 1)
                    SM.ti(E.tensor_scalar(out=wqo[:, 0:cols], in0=wpu[:, 0:cols],
                                          scalar1=CMAG, scalar2=None,
                                          op0=ALU.subtract), "vE", 1)

            @blk.scalar
            def _(E):
                for i, (src, roff, cols, agdst, agro, sco) in enumerate(wtiles):
                    SM.w(E, "vE", ve_wp0 + 2 * i + 1)
                    SM.ti(E.activation(out=wpu[:, 0:cols], in_=wps[:, 0:cols],
                                       func=AF.Identity,
                                       scale=tiny[:, C_INV + i:C_INV + i + 1],
                                       bias=tiny[:, 68:69]), "sE", 1)

            if upto == "wprep":
                finish_dummy()
                return nc

            @blk.gpsimd
            def _(E):
                SM.w(E, "dL", dl_modout)
                SM.ti(E.collective_compute(
                    "AllGather", ALU.bypass, replica_groups=PAIRS,
                    ins=[ag_mod_i], outs=[ag_mod_o]), "cC", 1)
                SM.w(E, "dL", dl_wpend)
                SM.ti(E.collective_compute(
                    "AllGather", ALU.bypass, replica_groups=ALL8,
                    ins=[ag_wg_i], outs=[ag_wg_o]), "cC", 1)
                SM.ti(E.collective_compute(
                    "AllGather", ALU.bypass, replica_groups=ALL8,
                    ins=[ag_wu_i], outs=[ag_wu_o]), "cC", 1)
                SM.ti(E.collective_compute(
                    "AllGather", ALU.bypass, replica_groups=ALL8,
                    ins=[ag_wd_i], outs=[ag_wd_o]), "cC", 1)
                SM.ti(E.collective_compute(
                    "AllGather", ALU.bypass, replica_groups=ALL8,
                    ins=[ag_sc_i], outs=[ag_sc_o]), "cC", 1)
            cc_mod = 1
            cc_wall = SM.c["cC"]

            @blk.sync
            def _(E):
                SM.w(E, "cC", cc_mod)
                SM.ti(E.dma_start(out=modA[:], in_=bass.AP(
                    ag_mod_o.tensor, 0, [[24, 128], [1, 24]])), "dL", 16)
                SM.ti(E.dma_start(out=modB[:], in_=bass.AP(
                    ag_mod_o.tensor, 128 * 24, [[24, 128], [1, 24]])), "dL", 16)
                SM.w(E, "cC", cc_wall)
                for r in range(8):
                    SM.ti(E.dma_start(
                        out=scales_all[:, r * 7:(r + 1) * 7],
                        in_=bass.AP(ag_sc_o.tensor, r * 896, [[1, 128], [128, 7]]),
                        allow_slow_non_contiguous=True), "dL", 16)
            dl_mod = SM.c["dL"]

            @blk.vector
            def _(E):
                SM.w(E, "dL", dl_mod)
                E.tensor_scalar(out=onepca[:], in0=modA[:, 8:16], scalar1=1.0,
                                scalar2=None, op0=ALU.add)
                E.tensor_scalar(out=onepcb[:], in0=modB[:, 8:16], scalar1=1.0,
                                scalar2=None, op0=ALU.add)
                sgA, sgB = tiny[:, T_SCR + 4:T_SCR + 12], vjnk[:, 0:8]
                abA, abB = tiny[:, T_SCR + 12:T_SCR + 20], vjnk[:, 8:16]
                for mod_t, sg in ((modA, sgA), (modB, sgB)):
                    E.tensor_scalar(out=sg, in0=mod_t[:, 16:24],
                                    scalar1=0.0, scalar2=2.0,
                                    op0=ALU.is_ge, op1=ALU.mult)
                for mod_t, sg in ((modA, sgA), (modB, sgB)):
                    E.tensor_scalar(out=sg, in0=sg, scalar1=1.0, scalar2=None,
                                    op0=ALU.subtract)
                for mod_t, ab in ((modA, abA), (modB, abB)):
                    E.tensor_scalar(out=ab, in0=mod_t[:, 16:24],
                                    scalar1=-1.0, scalar2=None, op0=ALU.mult)
                for mod_t, ab in ((modA, abA), (modB, abB)):
                    E.tensor_tensor(out=ab, in0=ab, in1=mod_t[:, 16:24],
                                    op=ALU.max)
                for mod_t, ab in ((modA, abA), (modB, abB)):
                    E.tensor_scalar(out=ab, in0=ab, scalar1=0.25, scalar2=None,
                                    op0=ALU.max)
                for mod_t, sg, ab in ((modA, sgA, abA), (modB, sgB, abB)):
                    E.tensor_tensor(out=mod_t[:, 16:24], in0=ab, in1=sg,
                                    op=ALU.mult)
                SM.ti(E.engine_nop(), "vE", 1)
            ve_modready = SM.c["vE"]
            tapdump("modA", modA[:], "vE", ve_modready)
            tapdump("modB", modB[:], "vE", ve_modready)

            @blk.tensor
            def _(E):
                gate(E)
                pass

        if upto == "p0":
            finish_dummy()
            return nc
        with ExitStack() as s1:
            bigC = s1.enter_context(nc.sbuf_tensor("bigC", [128, NCH * SL], BF16))
            with ExitStack() as s2:
                e2 = s2.enter_context
                bigA = e2(nc.sbuf_tensor("bigA", [128, NCH * SL], BF16))
                bigB = e2(nc.sbuf_tensor("bigB", [128, NCH * SL], BF16))
                alphaB = e2(nc.sbuf_tensor("alphaB", [128, SL], BF16))
                betaB = e2(nc.sbuf_tensor("betaB", [128, SL], BF16))

                with ExitStack() as s3:
                    bigD = s3.enter_context(
                        nc.sbuf_tensor("bigD", [128, NCH * SL], BF16))
                    dl_x0 = SM.c["dL"]

                    @blk.sync
                    def _(E):
                        SM.w(E, "dG", ld_xcast)
                        for j in range(NCH):
                            SM.ti(E.dma_start(out=ch(bigD, j), in_=bass.AP(
                                xbf_scr.tensor, j * 128, [[D, SL], [1, 128]]),
                                transpose=True), "dL", 16)
                    if upto == "x1":
                        finish_dummy()
                        return nc

                    with ExitStack() as sn:
                        psS = sn.enter_context(
                            nc.psum_tensor("psS", [128, 2048], F32))
                        psS2 = sn.enter_context(
                            nc.psum_tensor("psS2", [128, 2048], F32))

                        se_sq0 = SM.c["sE"]
                        te_st0 = SM.c["tE"]

                        @blk.scalar
                        def _(E):
                            SM.w(E, "tE", te_tr0)
                            SM.w(E, "vE", ve_cbf)
                            for j in range(NCH):
                                SM.w(E, "dL", dl_x0 + 16 * (j + 1))
                                if j >= 2:
                                    SM.w(E, "tE", te_st0 + j - 1)
                                SM.ti(E.activation(
                                    out=(bsc0 if j % 2 == 0 else bsc1)[:],
                                    in_=ch(bigD, j), func=AF.Square), "sE", 1)

                        @blk.tensor
                        def _(E):
                            SM.w(E, "vE", ve_mod)
                            for j in range(NCH):
                                SM.w(E, "sE", se_sq0 + j + 1)
                                for s4 in range(4):
                                    _last = E.matmul(psS2[0:1, s4 * 512:(s4 + 1) * 512],
                                             onesbf[:, 0:1],
                                             (bsc0 if j % 2 == 0 else bsc1)
                                             [:, s4 * 512:(s4 + 1) * 512],
                                             start=(j == 0), stop=(j == NCH - 1))
                                for s4 in range(4):
                                    _last = E.matmul(psS[0:1, s4 * 512:(s4 + 1) * 512],
                                             onesbf[:, 0:1],
                                             ch(bigD, j, s4 * 512, (s4 + 1) * 512),
                                             start=(j == 0), stop=(j == NCH - 1))
                                SM.ti(_last, "tE", 1)
                        te_ln1s = SM.c["tE"]

                        @blk.vector
                        def _(E):
                            SM.w(E, "tE", te_ln1s)
                            E.tensor_scalar(out=ROW_MEAN(), in0=psS[0:1, :],
                                            scalar1=1.0 / D, scalar2=None,
                                            op0=ALU.mult)
                            E.tensor_scalar(out=ROW_TMP(), in0=psS2[0:1, :],
                                            scalar1=1.0 / D, scalar2=None,
                                            op0=ALU.mult)
                            E.tensor_tensor(out=ROW_BETA(), in0=ROW_MEAN(),
                                            in1=ROW_MEAN(), op=ALU.mult)
                            SM.ti(E.tensor_tensor(out=ROW_TMP(), in0=ROW_TMP(),
                                                  in1=ROW_BETA(),
                                                  op=ALU.subtract), "vE", 1)
                        ve_var1 = SM.c["vE"]

                        @blk.scalar
                        def _(E):
                            SM.w(E, "vE", ve_var1)
                            SM.ti(E.activation(out=ROW_BETA(), in_=ROW_TMP(),
                                               func=AF.Sqrt,
                                               bias=tiny[0:1, 69:70]), "sE", 1)
                        se_std1 = SM.c["sE"]

                        @blk.vector
                        def _(E):
                            SM.w(E, "sE", se_std1)
                            E.reciprocal(out=ROW_ALPHA(), in_=ROW_BETA())
                            E.tensor_tensor(out=ROW_BETA(), in0=ROW_MEAN(),
                                            in1=ROW_ALPHA(), op=ALU.mult)
                            SM.ti(E.tensor_scalar(out=ROW_BETA(), in0=ROW_BETA(),
                                                  scalar1=-1.0, scalar2=None,
                                                  op0=ALU.mult), "vE", 1)
                        ve_ab1 = SM.c["vE"]

                        @blk.tensor
                        def _(E):
                            SM.w(E, "vE", ve_ab1)
                            for s4 in range(4):
                                _last = E.matmul(psS2[:, s4 * 512:(s4 + 1) * 512],
                                         ones32[0:1, :],
                                         rowA[0:1,
                                                 s4 * 512:(s4 + 1) * 512],
                                         start=True, stop=True)
                            SM.ti(_last, "tE", 1)
                        te_aB1 = SM.c["tE"]

                        @blk.scalar
                        def _(E):
                            SM.w(E, "tE", te_aB1)
                            SM.ti(E.activation(out=alphaB[:], in_=psS2[:],
                                               func=AF.Copy), "sE", 1)
                        se_aB1 = SM.c["sE"]

                        @blk.tensor
                        def _(E):
                            SM.w(E, "sE", se_aB1)
                            for s4 in range(4):
                                _last = E.matmul(psS2[:, s4 * 512:(s4 + 1) * 512],
                                         ones32[32:33, :],
                                         rowA[32:33,
                                                 s4 * 512:(s4 + 1) * 512],
                                         start=True, stop=True)
                            SM.ti(_last, "tE", 1)
                        te_bB1 = SM.c["tE"]

                        @blk.scalar
                        def _(E):
                            SM.w(E, "tE", te_bB1)
                            SM.ti(E.activation(out=betaB[:], in_=psS2[:],
                                               func=AF.Copy), "sE", 1)
                        se_bB1 = SM.c["sE"]
                        if upto == "ln1s":
                            finish_dummy()
                            return nc

                        ve_ap0 = SM.c["vE"]
                        se_af0 = SM.c["sE"]

                        @blk.vector
                        def _(E):
                            SM.w(E, "sE", se_bB1)
                            SM.w(E, "vE", ve_modready)
                            for j in range(NCH):
                                if j >= 2:
                                    SM.w(E, "sE", se_af0 + j - 1)
                                E.tensor_tensor(
                                    out=(bsc0 if j % 2 == 0 else bsc1)[:],
                                    in0=ch(bigD, j), in1=alphaB[:], op=ALU.mult)
                                SM.ti(E.tensor_tensor(
                                    out=(bsc0 if j % 2 == 0 else bsc1)[:],
                                    in0=(bsc0 if j % 2 == 0 else bsc1)[:],
                                    in1=betaB[:], op=ALU.add), "vE", 1)

                        @blk.scalar
                        def _(E):
                            for j in range(NCH):
                                SM.w(E, "vE", ve_ap0 + j + 1)
                                SM.ti(E.activation(
                                    out=ch(bigA, j),
                                    in_=(bsc0 if j % 2 == 0 else bsc1)[:],
                                    func=AF.Identity, scale=onepca[:, j:j + 1],
                                    bias=modA[:, j:j + 1]), "sE", 1)
                        se_h = SM.c["sE"]
                        tapdump("h", bigA[:], "sE", se_h)
                        if upto == "h":
                            finish_dummy()
                            return nc

                        se_sq2 = SM.c["sE"]
                        te_rs0 = SM.c["tE"]

                        @blk.scalar
                        def _(E):
                            for j in range(NCH):
                                if j >= 2:
                                    SM.w(E, "tE", te_rs0 + j - 1)
                                SM.ti(E.activation(
                                    out=(bsc0 if j % 2 == 0 else bsc1)[:],
                                    in_=ch(bigA, j), func=AF.Square), "sE", 1)

                        @blk.tensor
                        def _(E):
                            for j in range(NCH):
                                SM.w(E, "sE", se_sq2 + j + 1)
                                for s4 in range(4):
                                    _last = E.matmul(psS[0:1, s4 * 512:(s4 + 1) * 512],
                                             onesbf[:, 0:1],
                                             (bsc0 if j % 2 == 0 else bsc1)
                                             [:, s4 * 512:(s4 + 1) * 512],
                                             start=(j == 0), stop=(j == NCH - 1))
                                SM.ti(_last, "tE", 1)
                        te_rms = SM.c["tE"]

                        @blk.vector
                        def _(E):
                            SM.w(E, "tE", te_rms)
                            SM.ti(E.tensor_scalar(
                                out=ROW_TMP(), in0=psS[0:1, :],
                                scalar1=1.0 / D, scalar2=None, op0=ALU.mult),
                                "vE", 1)
                        ve_ms2 = SM.c["vE"]

                        @blk.scalar
                        def _(E):
                            SM.w(E, "vE", ve_ms2)
                            SM.ti(E.activation(out=ROW_BETA(),
                                               in_=ROW_TMP(),
                                               func=AF.Sqrt, bias=tiny[0:1, 70:71]),
                                  "sE", 1)
                        se_rms2 = SM.c["sE"]

                        @blk.vector
                        def _(E):
                            SM.w(E, "sE", se_rms2)
                            SM.ti(E.reciprocal(
                                out=ROW_ALPHA(),
                                in_=ROW_BETA()), "vE", 1)
                        ve_ir2 = SM.c["vE"]

                        @blk.tensor
                        def _(E):
                            SM.w(E, "vE", ve_ir2)
                            for s4 in range(4):
                                _last = E.matmul(psS2[:, s4 * 512:(s4 + 1) * 512],
                                         ones32[0:1, :],
                                         rowA[0:1,
                                                 s4 * 512:(s4 + 1) * 512],
                                         start=True, stop=True)
                            SM.ti(_last, "tE", 1)
                        te_iB2 = SM.c["tE"]

                        @blk.scalar
                        def _(E):
                            SM.w(E, "tE", te_iB2)
                            SM.ti(E.activation(out=alphaB[:], in_=psS2[:],
                                               func=AF.Copy), "sE", 1)
                        se_iB2 = SM.c["sE"]

                        ve_xn0 = SM.c["vE"]
                        se_xn0 = SM.c["sE"]

                        @blk.vector
                        def _(E):
                            SM.w(E, "sE", se_iB2)
                            for j in range(NCH):
                                if j >= 2:
                                    SM.w(E, "sE", se_xn0 + j - 1)
                                SM.ti(E.tensor_tensor(
                                    out=(bsc0 if j % 2 == 0 else bsc1)[:],
                                    in0=ch(bigA, j), in1=alphaB[:], op=ALU.mult),
                                    "vE", 1)

                        @blk.scalar
                        def _(E):
                            for j in range(NCH):
                                SM.w(E, "vE", ve_xn0 + j + 1)
                                SM.ti(E.activation(
                                    out=ch(bigB, j),
                                    in_=(bsc0 if j % 2 == 0 else bsc1)[:],
                                    func=AF.Copy, scale=wni_pp[:, j:j + 1],
                                    accum_out=routeacc[:, j:j + 1]), "sE", 1)
                        se_xn = SM.c["sE"]
                        tapdump("xn", bigB[:], "sE", se_xn)

                        @blk.tensor
                        def _(E):
                            gate(E)
                        if upto == "xn":
                            finish_dummy()
                            return nc

                ve_s10 = SM.c["vE"]

                @blk.vector
                def _(E):
                    for j in range(NCH):
                        SM.w(E, "sE", se_xn - NCH + j + 1)
                        E.tensor_scalar(out=fsc0[:], in0=ch(bigB, j),
                                        scalar1=omlam_pp[:, j:j + 1],
                                        scalar2=None, op0=ALU.mult)
                        E.tensor_tensor_scan(
                            out=fsc1[:],
                            data0=bass.AP(lam_pp, j, [[8, 128], [0, SL]]),
                            data1=fsc0[:], initial=tiny[:, T_ZERO:T_ZERO + 1],
                            op0=ALU.mult, op1=ALU.add)
                        vg(E)
                        vg(E)
                        SM.ti(E.tensor_copy(out=emafin[:, j:j + 1],
                                            in_=fsc1[:, SL - 1:SL]), "vE", 1)
                ve_scan1 = SM.c["vE"]

                @blk.sync
                def _(E):
                    SM.w(E, "vE", ve_scan1)
                    SM.w(E, "sE", se_xn)
                    SM.ti(E.dma_start(out=bass.AP(ag_re_i.tensor, 0,
                                                  [[8, 128], [1, 8]]),
                                      in_=routeacc[:]), "dL", 16)
                    SM.ti(E.dma_start(out=bass.AP(ag_re_i.tensor, 128 * 8,
                                                  [[8, 128], [1, 8]]),
                                      in_=emafin[:]), "dL", 16)
                dl_re = SM.c["dL"]

                @blk.gpsimd
                def _(E):
                    SM.w(E, "dL", dl_re)
                    SM.ti(E.collective_compute(
                        "AllGather", ALU.bypass, replica_groups=PAIRS,
                        ins=[ag_re_i], outs=[ag_re_o]), "cC", 1)
                cc_re = SM.c["cC"]

                @blk.sync
                def _(E):
                    SM.w(E, "cC", cc_re)
                    SM.ti(E.dma_start(out=routeacc[:], in_=bass.AP(
                        ag_re_o.tensor, 0, [[8, 128], [1, 8]])), "dL", 16)
                    SM.ti(E.dma_start(out=emafin[:], in_=bass.AP(
                        ag_re_o.tensor, 256 * 8, [[8, 128], [1, 8]])), "dL", 16)
                    SM.ti(E.dma_start(out=carry_sb[:], in_=bass.AP(
                        ag_re_o.tensor, 128 * 8, [[8, 128], [1, 8]])), "dL", 16)
                dl_rd = SM.c["dL"]

                @blk.vector
                def _(E):
                    SM.w(E, "dL", dl_rd)
                    vg(E)
                    E.tensor_tensor(out=routeacc[:], in0=routeacc[:],
                                    in1=emafin[:], op=ALU.add)
                    E.tensor_scalar(out=carry_sb[:], in0=carry_sb[:],
                                    scalar1=flag_pp[:, 0:1], scalar2=None,
                                    op0=ALU.mult)
                    SM.ti(E.tensor_scalar(out=routeacc[:], in0=routeacc[:],
                                          scalar1=1.0 / float(S), scalar2=None,
                                          op0=ALU.mult), "vE", 1)
                ve_carry = SM.c["vE"]
                tapdump("route", routeacc[:], "vE", ve_carry)
                tapdump("carry", carry_sb[:], "vE", ve_carry)

                @blk.vector
                def _(E):
                    SM.w(E, "vE", ve_carry)
                    for j in range(NCH):
                        E.tensor_scalar(out=fsc0[:], in0=ch(bigB, j),
                                        scalar1=omlam_pp[:, j:j + 1],
                                        scalar2=None, op0=ALU.mult)
                        SM.ti(E.tensor_tensor_scan(
                            out=ch(bigC, j),
                            data0=bass.AP(lam_pp, j, [[8, 128], [0, SL]]),
                            data1=fsc0[:], initial=carry_sb[:, j:j + 1],
                            op0=ALU.mult, op1=ALU.add), "vE", 1)
                ve_scan2 = SM.c["vE"]
                tapdump("ssm", bigC[:], "vE", ve_scan2)
                if upto == "scan2":
                    finish_dummy()
                    return nc

                with ExitStack() as s4:
                    e4 = s4.enter_context
                    qk_sb = e4(nc.sbuf_tensor("qk_sb", [128, SL], BF16))
                    kT = e4(nc.sbuf_tensor("kT", [64, 2 * SL], BF16))
                    vaug = e4(nc.sbuf_tensor("vaug", [128, 32 * 33], BF16))
                    e_sb = e4(nc.sbuf_tensor("e_sb", [128, 2 * 1024], BF16))
                    ctx_sb = e4(nc.sbuf_tensor("ctx_sb", [32, SL], BF16))
                    sub_v = ExitStack()
                    vTf = sub_v.enter_context(
                        nc.sbuf_tensor("vTf", [32, 2 * SL], BF16))
                    vt_sb = e_sb[0:32, 0:SL]

                    with ExitStack() as pp:
                        psQK = pp.enter_context(
                            nc.psum_tensor("psQK", [128, 2048], F32))
                        psV = pp.enter_context(
                            nc.psum_tensor("psV", [32, 2048], F32))

                        @blk.tensor
                        def _(E):
                            SM.w(E, "sE", se_xn)
                            for j in range(NCH):
                                for s4_ in range(4):
                                    _last = E.matmul(psQK[:, s4_ * 512:(s4_ + 1) * 512],
                                             wqk_sb[:, j * 128:(j + 1) * 128],
                                             ch(bigB, j, s4_ * 512, (s4_ + 1) * 512),
                                             start=(j == 0), stop=(j == NCH - 1))
                            for j in range(NCH):
                                for s4_ in range(4):
                                    _last = E.matmul(psV[:, s4_ * 512:(s4_ + 1) * 512],
                                             wv_sb[:, j * 32:(j + 1) * 32],
                                             ch(bigB, j, s4_ * 512, (s4_ + 1) * 512),
                                             start=(j == 0), stop=(j == NCH - 1))
                            SM.ti(_last, "tE", 1)
                        te_proj = SM.c["tE"]

                        @blk.scalar
                        def _(E):
                            SM.w(E, "tE", te_proj)
                            E.activation(out=qk_sb[:], in_=psQK[:], func=AF.Copy)
                            SM.ti(E.activation(out=vt_sb, in_=psV[:],
                                               func=AF.Copy), "sE", 1)
                        se_proj = SM.c["sE"]
                        tapdump("qk", qk_sb[:], "sE", se_proj)

                        @blk.tensor
                        def _(E):
                            gate(E)

                    @blk.sync
                    def _(E):
                        SM.w(E, "sE", se_proj)
                        SM.ti(E.dma_start(out=bass.AP(ag_kv_i.tensor, 0,
                                                      [[SL, 32], [1, SL]]),
                                          in_=qk_sb[64:96, :]), "dL", 16)
                        SM.ti(E.dma_start(out=bass.AP(ag_kv_i.tensor, 32 * SL,
                                                      [[SL, 32], [1, SL]]),
                                          in_=qk_sb[96:128, :]), "dL", 16)
                        SM.ti(E.dma_start(out=bass.AP(ag_kv_i.tensor, 64 * SL,
                                                      [[SL, 32], [1, SL]]),
                                          in_=vt_sb), "dL", 16)
                    dl_kv = SM.c["dL"]

                    @blk.gpsimd
                    def _(E):
                        SM.w(E, "dL", dl_kv)
                        SM.ti(E.collective_compute(
                            "AllGather", ALU.bypass, replica_groups=PAIRS,
                            ins=[ag_kv_i], outs=[ag_kv_o]), "cC", 1)
                    cc_kv = SM.c["cC"]

                    @blk.sync
                    def _(E):
                        SM.w(E, "cC", cc_kv)
                        for half in range(2):
                            base = half * 96 * SL
                            SM.ti(E.dma_start(
                                out=kT[0:32, half * SL:(half + 1) * SL],
                                in_=bass.AP(ag_kv_o.tensor, base,
                                            [[SL, 32], [1, SL]])), "dL", 16)
                            SM.ti(E.dma_start(
                                out=kT[32:64, half * SL:(half + 1) * SL],
                                in_=bass.AP(ag_kv_o.tensor, base + 32 * SL,
                                            [[SL, 32], [1, SL]])), "dL", 16)
                            SM.ti(E.dma_start(
                                out=vTf[:, half * SL:(half + 1) * SL],
                                in_=bass.AP(ag_kv_o.tensor, base + 64 * SL,
                                            [[SL, 32], [1, SL]])), "dL", 16)
                    dl_kvload = SM.c["dL"]

                    with ExitStack() as pv:
                        psVA = pv.enter_context(
                            nc.psum_tensor("psVA", [128, 1024], BF16))

                        @blk.tensor
                        def _(E):
                            SM.w(E, "dL", dl_kvload)
                            for k in range(32):
                                _last = E.transpose(psVA[:, k * 32:(k + 1) * 32],
                                            vTf[:, k * 128:(k + 1) * 128],
                                            identbf[0:32, 0:32])
                                E.nop()
                            SM.ti(_last, "tE", 1)
                        te_va = SM.c["tE"]

                        @blk.vector
                        def _(E):
                            SM.w(E, "vE", ve_scan2)
                            E.memset(bass.AP(vaug, 32, [[1056, 128], [33, 32]]), 1.0)
                            SM.w(E, "tE", te_va)
                            SM.ti(E.tensor_copy(
                                out=bass.AP(vaug, 0, [[1056, 128], [33, 32], [1, 32]]),
                                in_=bass.AP(psVA, 0, [[1024, 128], [32, 32], [1, 32]])),
                                "vE", 1)
                        ve_vaug = SM.c["vE"]

                        @blk.tensor
                        def _(E):
                            gate(E)
                    sub_v.close()
                    if upto == "kv":
                        finish_dummy()
                        return nc
                    cs1 = e4(nc.sbuf_tensor("cs1", [33, SL], F32))
                    cs2 = e4(nc.sbuf_tensor("cs2", [33, SL], F32))

                    with ExitStack() as pq:
                        psQN = pq.enter_context(
                            nc.psum_tensor("psQN", [128, 2048], F32))

                        se_qk0 = SM.c["sE"]
                        te_qk0 = SM.c["tE"]
                        ve_qk0 = SM.c["vE"]

                        @blk.scalar
                        def _(E):
                            SM.w(E, "sE", se_proj)
                            SM.ti(E.activation(out=bsc0[:], in_=qk_sb[:],
                                               func=AF.Square), "sE", 1)
                            SM.w(E, "dL", dl_kvload)
                            SM.ti(E.activation(out=bsc1[0:64, :], in_=kT[:, 0:SL],
                                               func=AF.Square), "sE", 1)
                            SM.w(E, "tE", te_qk0 + 1)
                            SM.ti(E.activation(out=bsc0[0:64, :],
                                               in_=kT[:, SL:2 * SL],
                                               func=AF.Square), "sE", 1)

                        @blk.tensor
                        def _(E):
                            SM.w(E, "vE", ve_vaug)
                            SM.w(E, "sE", se_qk0 + 1)
                            for s4_ in range(4):
                                _last = E.matmul(psQN[0:4, s4_ * 512:(s4_ + 1) * 512],
                                         grp4bf[:],
                                         bsc0[:, s4_ * 512:(s4_ + 1) * 512],
                                         start=True, stop=True)
                            SM.ti(_last, "tE", 1)
                            SM.w(E, "sE", se_qk0 + 2)
                            SM.w(E, "vE", ve_qk0 + 1)
                            for s4_ in range(4):
                                _last = E.matmul(psQN[0:2, s4_ * 512:(s4_ + 1) * 512],
                                         grp4bf[0:64, 0:2],
                                         bsc1[0:64, s4_ * 512:(s4_ + 1) * 512],
                                         start=True, stop=True)
                            SM.ti(_last, "tE", 1)
                            SM.w(E, "sE", se_qk0 + 3)
                            SM.w(E, "vE", ve_qk0 + 2)
                            for s4_ in range(4):
                                _last = E.matmul(psQN[0:2, s4_ * 512:(s4_ + 1) * 512],
                                         grp4bf[0:64, 0:2],
                                         bsc0[0:64, s4_ * 512:(s4_ + 1) * 512],
                                         start=True, stop=True)
                            SM.ti(_last, "tE", 1)
                        te_qn = SM.c["tE"]

                        @blk.vector
                        def _(E):
                            SM.w(E, "tE", te_qk0 + 1)
                            SM.ti(E.reduce_max(out=tiny[0:4, T_SCR:T_SCR + 1],
                                               in_=psQN[0:4, :], axis=AX), "vE", 1)
                            SM.w(E, "tE", te_qk0 + 2)
                            SM.ti(E.reduce_max(out=tiny[0:2, T_SCR + 1:T_SCR + 2],
                                               in_=psQN[0:2, :], axis=AX), "vE", 1)
                            SM.w(E, "tE", te_qk0 + 3)
                            SM.ti(E.reduce_max(out=tiny[0:2, T_SCR + 2:T_SCR + 3],
                                               in_=psQN[0:2, :], axis=AX), "vE", 1)
                            vg(E)
                            E.tensor_tensor(out=tiny[0:2, T_SCR + 3:T_SCR + 4],
                                            in0=tiny[0:2, T_SCR + 1:T_SCR + 2],
                                            in1=tiny[0:2, T_SCR + 2:T_SCR + 3],
                                            op=ALU.max)
                            vg(E)
                            SM.ti(E.tensor_tensor(
                                out=tiny[0:2, T_SCR + 4:T_SCR + 5],
                                in0=tiny[0:2, T_SCR:T_SCR + 1],
                                in1=tiny[0:2, T_SCR + 3:T_SCR + 4],
                                op=ALU.mult), "vE", 1)
                        ve_qn = SM.c["vE"]

                        @blk.tensor
                        def _(E):
                            SM.w(E, "vE", ve_qn)
                            SM.ti(E.transpose(psQN[0:1, 1024:1026],
                                              tiny[0:2, T_SCR + 4:T_SCR + 5],
                                              ident32[0:2, 0:2]), "tE", 1)
                        te_qnt = SM.c["tE"]

                        @blk.vector
                        def _(E):
                            SM.w(E, "tE", te_qnt)
                            r = rowB[0:1, :]
                            SM.ti(E.reduce_max(out=r[:, 10:11],
                                               in_=psQN[0:1, 1024:1026],
                                               axis=AX), "vE", 1)
                        ve_cpre = SM.c["vE"]

                        @blk.scalar
                        def _(E):
                            SM.w(E, "vE", ve_cpre)
                            SM.ti(E.activation(out=rowB[0:1, 11:12],
                                               in_=rowB[0:1, 10:11],
                                               func=AF.Sqrt), "sE", 1)
                        se_cb = SM.c["sE"]

                        @blk.vector
                        def _(E):
                            SM.w(E, "sE", se_cb)
                            SM.ti(E.tensor_scalar(
                                out=rowB[0:1, 11:12],
                                in0=rowB[0:1, 11:12],
                                scalar1=-SC_ATTN, scalar2=None, op0=ALU.mult),
                                "vE", 1)
                        ve_cneg = SM.c["vE"]

                        @blk.tensor
                        def _(E):
                            SM.w(E, "vE", ve_cneg)
                            SM.ti(E.matmul(psQN[:, 1032:1033], ones32[0:1, 0:128],
                                           rowB[0:1, 11:12],
                                           start=True, stop=True), "tE", 1)
                        te_cb = SM.c["tE"]

                        @blk.vector
                        def _(E):
                            SM.w(E, "tE", te_cb)
                            SM.ti(E.tensor_copy(out=tiny[:, T_NC:T_NC + 1],
                                                in_=psQN[:, 1032:1033]), "vE", 1)
                        ve_cready = SM.c["vE"]

                        @blk.tensor
                        def _(E):
                            gate(E)

                    with ExitStack() as pr:
                        psR = pr.enter_context(nc.psum_tensor("psR", [128, 512], F32))

                        @blk.vector
                        def _(E):
                            SM.w(E, "vE", ve_carry)
                            SM.ti(E.tensor_copy(out=tinyb[:, 0:8], in_=routeacc[:]),
                                  "vE", 1)
                        ve_rbf = SM.c["vE"]

                        @blk.tensor
                        def _(E):
                            SM.w(E, "vE", ve_rbf)
                            for j in range(NCH):
                                _last = E.matmul(psR[0:1, 0:512], tinyb[:, j:j + 1],
                                         rw1_sb[:, j * 512:(j + 1) * 512],
                                         start=(j == 0), stop=(j == NCH - 1))
                            SM.ti(_last, "tE", 1)
                        te_rt1 = SM.c["tE"]

                        @blk.scalar
                        def _(E):
                            SM.w(E, "tE", te_rt1)
                            SM.ti(E.activation(out=fsc0[0:1, 0:512],
                                               in_=psR[0:1, 0:512], func=AF.Silu),
                                  "sE", 1)
                        se_rt = SM.c["sE"]

                        @blk.tensor
                        def _(E):
                            SM.w(E, "sE", se_rt)
                            for q4 in range(4):
                                _last = E.transpose(psR[:, 128 + q4:129 + q4],
                                            fsc0[0:1, q4 * 128:(q4 + 1) * 128],
                                            ident32[0:1, 0:1])
                                E.nop()
                            SM.ti(_last, "tE", 1)
                        te_rt2 = SM.c["tE"]

                        @blk.vector
                        def _(E):
                            SM.w(E, "tE", te_rt2)
                            SM.ti(E.tensor_copy(out=tinyb[:, 8:12],
                                                in_=psR[:, 128:132]), "vE", 1)
                        ve_rt2 = SM.c["vE"]

                        @blk.tensor
                        def _(E):
                            SM.w(E, "vE", ve_rt2)
                            for q4 in range(4):
                                _last = E.matmul(psR[0:1, 132:135], tinyb[:, 8 + q4:9 + q4],
                                         rw2_sb[:, q4 * 3:(q4 + 1) * 3],
                                         start=(q4 == 0), stop=(q4 == 3))
                            SM.ti(_last, "tE", 1)
                        te_rt3 = SM.c["tE"]

                        @blk.vector
                        def _(E):
                            SM.w(E, "tE", te_rt3)
                            r = rowB[0:1, :]
                            E.tensor_tensor(out=r[:, 0:3], in0=psR[0:1, 132:135],
                                            in1=rb2_sb[0:1, 0:3], op=ALU.add)
                            vg(E)
                            E.reduce_max(out=r[:, 3:4], in_=r[:, 0:3], axis=AX)
                            vg(E)
                            SM.ti(E.tensor_scalar(out=r[:, 8:9], in0=r[:, 3:4],
                                                  scalar1=-1.0, scalar2=None,
                                                  op0=ALU.mult), "vE", 1)
                        ve_rt3 = SM.c["vE"]

                        @blk.scalar
                        def _(E):
                            SM.w(E, "vE", ve_rt3)
                            r = rowB[0:1, :]
                            SM.ti(E.activation(out=r[:, 4:7], in_=r[:, 0:3],
                                               func=AF.Exp, bias=r[:, 8:9]), "sE", 1)
                        se_rex = SM.c["sE"]

                        @blk.vector
                        def _(E):
                            SM.w(E, "sE", se_rex)
                            r = rowB[0:1, :]
                            E.reduce_sum(out=r[:, 9:10], in_=r[:, 4:7], axis=AX)
                            E.tensor_tensor(out=r[:, 11:12], in0=r[:, 5:6],
                                            in1=r[:, 6:7], op=ALU.add)
                            E.reciprocal(out=r[:, 10:11], in_=r[:, 9:10])
                            vg(E)
                            E.tensor_tensor(out=r[:, 11:12], in0=r[:, 11:12],
                                            in1=r[:, 10:11], op=ALU.mult)
                            E.tensor_tensor(out=r[:, 13:14], in0=r[:, 6:7],
                                            in1=r[:, 10:11], op=ALU.mult)
                            E.tensor_scalar(out=tiny[0:1, X_NPM:X_NPM + 1],
                                            in0=r[:, 11:12],
                                            scalar1=-1.0, scalar2=None, op0=ALU.mult)
                            SM.ti(E.tensor_tensor(out=tiny[0:1, X_PF:X_PF + 1],
                                                  in0=r[:, 13:14],
                                                  in1=flag_pp[0:1, 0:1],
                                                  op=ALU.mult), "vE", 1)
                        ve_probs = SM.c["vE"]

                        @blk.tensor
                        def _(E):
                            SM.w(E, "vE", ve_probs)
                            _last = E.matmul(psR[:, 136:137], ones32[0:1, 0:128],
                                     tiny[0:1, X_NPM:X_NPM + 1],
                                     start=True, stop=True)
                            SM.ti(E.matmul(psR[:, 137:138], ones32[0:1, 0:128],
                                           tiny[0:1, X_PF:X_PF + 1],
                                           start=True, stop=True), "tE", 1)
                        te_pb = SM.c["tE"]

                        @blk.vector
                        def _(E):
                            SM.w(E, "tE", te_pb)
                            E.tensor_copy(out=tiny[:, T_PPM:T_PPM + 1],
                                          in_=psR[:, 136:137])
                            SM.ti(E.tensor_copy(out=tiny[:, T_PF:T_PF + 1],
                                                in_=psR[:, 137:138]), "vE", 1)
                        ve_pb = SM.c["vE"]
                        tapdump("pcols", tiny[:], "vE", ve_pb)

                        @blk.tensor
                        def _(E):
                            gate(E)

                    with ExitStack() as pa:
                        psSc = pa.enter_context(
                            nc.psum_tensor("psSc", [128, 2048], F32))
                        psC1 = pa.enter_context(
                            nc.psum_tensor("psC1", [33, 512], F32))
                        psC2 = pa.enter_context(
                            nc.psum_tensor("psC2", [33, 512], F32))

                        te_at0 = SM.c["tE"]
                        se_at0 = SM.c["sE"]
                        ve_at0 = SM.c["vE"]

                        @blk.tensor
                        def _(E):
                            SM.w(E, "vE", ve_cready)
                            SM.w(E, "vE", ve_vaug)
                            SM.w(E, "vE", ve_pb)
                            for qt in range(4):
                                for kt in range(32):
                                    buf = kt % 2
                                    if kt >= 2:
                                        SM.w(E, "sE",
                                             se_at0 + 32 * qt + (kt - 2) + 1)
                                    elif qt > 0:
                                        SM.w(E, "sE", se_at0 + 32 * qt)
                                    _last = E.matmul(psSc[:, buf * 1024:buf * 1024 + 512],
                                             kT[0:32, kt * 128:(kt + 1) * 128],
                                             qk_sb[0:32, qt * 512:(qt + 1) * 512],
                                             start=True, stop=True)
                                    _last = E.matmul(psSc[:, buf * 1024 + 512:buf * 1024 + 1024],
                                             kT[32:64, kt * 128:(kt + 1) * 128],
                                             qk_sb[32:64, qt * 512:(qt + 1) * 512],
                                             start=True, stop=True)
                                    SM.ti(_last, "tE", 1)
                                    SM.w(E, "sE", se_at0 + 32 * qt + kt + 1)
                                    if kt == 0 and qt > 0:
                                        SM.w(E, "vE", ve_at0 + qt)
                                    _last = E.matmul(psC1[:, 0:512],
                                             vaug[:, kt * 33:kt * 33 + 33],
                                             e_sb[:, buf * 1024:buf * 1024 + 512],
                                             start=(kt == 0), stop=(kt == 31))
                                    _last = E.matmul(psC2[:, 0:512],
                                             vaug[:, kt * 33:kt * 33 + 33],
                                             e_sb[:, buf * 1024 + 512:buf * 1024 + 1024],
                                             start=(kt == 0), stop=(kt == 31))
                                    SM.ti(_last, "tE", 1)
                        te_atend = SM.c["tE"]

                        @blk.scalar
                        def _(E):
                            for qt in range(4):
                                for kt in range(32):
                                    buf = kt % 2
                                    SM.w(E, "tE", te_at0 + 64 * qt + 2 * kt + 1)
                                    SM.ti(E.activation(
                                        out=e_sb[:, buf * 1024:buf * 1024 + 1024],
                                        in_=psSc[:, buf * 1024:buf * 1024 + 1024],
                                        func=AF.Exp, scale=tiny[:, 71:72],
                                        bias=tiny[:, T_NC:T_NC + 1]), "sE", 1)
                        se_atend = SM.c["sE"]

                        @blk.vector
                        def _(E):
                            for qt in range(4):
                                SM.w(E, "tE", te_at0 + 64 * qt + 64)
                                E.tensor_copy(out=cs1[:, qt * 512:(qt + 1) * 512],
                                              in_=psC1[:, 0:512])
                                SM.ti(E.tensor_copy(
                                    out=cs2[:, qt * 512:(qt + 1) * 512],
                                    in_=psC2[:, 0:512]), "vE", 1)
                        ve_cse = SM.c["vE"]

                        @blk.vector
                        def _(E):
                            SM.w(E, "vE", ve_cse)
                            E.reciprocal(out=rowA[0:1, :],
                                         in_=cs1[32:33, :])
                            SM.ti(E.reciprocal(out=rowA[32:33, :],
                                               in_=cs2[32:33, :]), "vE", 1)
                        ve_rec = SM.c["vE"]

                        @blk.tensor
                        def _(E):
                            SM.w(E, "vE", ve_rec)
                            for s4_ in range(4):
                                _last = E.matmul(psSc[0:32, s4_ * 512:(s4_ + 1) * 512],
                                         ones32[0:1, 0:32],
                                         rowA[0:1,
                                                 s4_ * 512:(s4_ + 1) * 512],
                                         start=True, stop=True)
                            for s4_ in range(4):
                                _last = E.matmul(psSc[64:96, s4_ * 512:(s4_ + 1) * 512],
                                         ones32[32:33, 0:32],
                                         rowA[32:33,
                                                 s4_ * 512:(s4_ + 1) * 512],
                                         start=True, stop=True)
                            SM.ti(_last, "tE", 1)
                        te_dbc = SM.c["tE"]

                        @blk.vector
                        def _(E):
                            SM.w(E, "tE", te_dbc)
                            E.tensor_tensor(out=fsc0[0:32, :], in0=cs1[0:32, :],
                                            in1=psSc[0:32, :], op=ALU.mult)
                            E.tensor_tensor(out=fsc1[0:32, :], in0=cs2[0:32, :],
                                            in1=psSc[64:96, :], op=ALU.mult)
                            SM.ti(E.scalar_tensor_tensor(
                                out=ctx_sb[:], in0=fsc1[0:32, :],
                                scalar=slam_pp[0:32, 0:1], in1=fsc0[0:32, :],
                                op0=ALU.mult, op1=ALU.subtract), "vE", 1)
                        ve_ctx = SM.c["vE"]
                        tapdump("ctx", ctx_sb[:], "vE", ve_ctx)
                        tapdump("den", cs1[:], "vE", ve_ctx)

                        @blk.tensor
                        def _(E):
                            gate(E)
                        if upto == "ctx":
                            finish_dummy()
                            return nc

                    with ExitStack() as pl:
                        psL = pl.enter_context(nc.psum_tensor("psL", [128, 512], F32))

                        @blk.tensor
                        def _(E):
                            SM.w(E, "vE", ve_ctx)
                            for j in range(NCH):
                                _last = E.matmul(psL[0:1, 0:64],
                                         ch(bigB, j, SL - 1, SL),
                                         lmT[:, j * SLOTS:(j + 1) * SLOTS],
                                         start=(j == 0), stop=(j == NCH - 1))
                            SM.ti(_last, "tE", 1)
                        te_lm1 = SM.c["tE"]

                        @blk.vector
                        def _(E):
                            SM.w(E, "tE", te_lm1)
                            r = rowB[32:33, :]
                            E.reduce_max(out=r[:, 64:65], in_=psL[0:1, 0:64],
                                         axis=AX)
                            vg(E)
                            SM.ti(E.tensor_scalar(out=r[:, 65:66], in0=r[:, 64:65],
                                                  scalar1=-SC_LM, scalar2=None,
                                                  op0=ALU.mult), "vE", 1)
                        ve_lm1 = SM.c["vE"]

                        @blk.scalar
                        def _(E):
                            SM.w(E, "vE", ve_lm1)
                            SM.ti(E.activation(out=rowB[32:33, 0:64],
                                               in_=psL[0:1, 0:64], func=AF.Exp,
                                               scale=tiny[0:1, 72:73],
                                               bias=rowB[32:33, 65:66]),
                                  "sE", 1)
                        se_lm = SM.c["sE"]

                        @blk.vector
                        def _(E):
                            SM.w(E, "sE", se_lm)
                            r = rowB[32:33, :]
                            E.reduce_sum(out=r[:, 66:67], in_=r[:, 0:64], axis=AX)
                            vg(E)
                            E.reciprocal(out=r[:, 67:68], in_=r[:, 66:67])
                            vg(E)
                            SM.ti(E.tensor_scalar(out=r[:, 0:64], in0=r[:, 0:64],
                                                  scalar1=r[:, 67:68], scalar2=None,
                                                  op0=ALU.mult), "vE", 1)
                        ve_lm2 = SM.c["vE"]

                        @blk.sync
                        def _(E):
                            SM.w(E, "vE", ve_lm2)
                            SM.ti(E.dma_start(out=lm_scr,
                                              in_=rowB[32:33, 0:64]),
                                  "dL", 16)
                            SM.ti(E.dma_start(out=tiny[0:64, T_SCR:T_SCR + 1],
                                              in_=bass.AP(lm_scr.tensor, 0,
                                                          [[1, 64], [1, 1]])),
                                  "dL", 16)
                        dl_lm = SM.c["dL"]

                        @blk.vector
                        def _(E):
                            SM.w(E, "dL", dl_lm)
                            SM.ti(E.tensor_copy(out=tinyb[0:64, 12:13],
                                                in_=tiny[0:64, T_SCR:T_SCR + 1]),
                                  "vE", 1)
                        ve_lmc = SM.c["vE"]

                        @blk.tensor
                        def _(E):
                            SM.w(E, "vE", ve_lmc)
                            for j in range(NCH):
                                _last = E.matmul(psL[:, 64 + j:65 + j],
                                         lm_nat[:, j * 128:(j + 1) * 128],
                                         tinyb[0:64, 12:13],
                                         start=True, stop=True)
                            SM.ti(_last, "tE", 1)
                        te_arc = SM.c["tE"]

                        @blk.vector
                        def _(E):
                            SM.w(E, "tE", te_arc)
                            SM.ti(E.tensor_copy(out=tiny[:, T_SCR + 4:T_SCR + 12],
                                                in_=psL[:, 64:72]), "vE", 1)
                        ve_arc = SM.c["vE"]

                        @blk.tensor
                        def _(E):
                            gate(E)

                    with ExitStack() as pw:
                        psW0 = pw.enter_context(
                            nc.psum_tensor("psW0", [128, 2048], F32))
                        psW1 = pw.enter_context(
                            nc.psum_tensor("psW1", [128, 2048], F32))
                        te_wo0 = SM.c["tE"]
                        ve_wo0 = SM.c["vE"]

                        @blk.tensor
                        def _(E):
                            SM.w(E, "vE", ve_ctx)
                            SM.w(E, "vE", ve_arc)
                            for j in range(NCH):
                                if j >= 2:
                                    SM.w(E, "vE", ve_wo0 + j - 1)
                                pw_ = psW0 if j % 2 == 0 else psW1
                                for s4_ in range(4):
                                    _last = E.matmul(pw_[:, s4_ * 512:(s4_ + 1) * 512],
                                             wo_sb[:, j * 128:(j + 1) * 128],
                                             ctx_sb[:, s4_ * 512:(s4_ + 1) * 512],
                                             start=True, stop=True)
                                SM.ti(_last, "tE", 1)

                        @blk.vector
                        def _(E):
                            for j in range(NCH):
                                SM.w(E, "tE", te_wo0 + j + 1)
                                pw_ = psW0 if j % 2 == 0 else psW1
                                SM.ti(E.scalar_tensor_tensor(
                                    out=ch(bigC, j), in0=pw_[:],
                                    scalar=tiny[:, T_PPM:T_PPM + 1],
                                    in1=ch(bigC, j), op0=ALU.mult, op1=ALU.add),
                                    "vE", 1)
                            SM.w(E, "vE", ve_arc)
                            for j in range(NCH):
                                E.tensor_tensor(
                                    out=vjnk[:, 8 + j:9 + j],
                                    in0=tiny[:, T_SCR + 4 + j:T_SCR + 5 + j],
                                    in1=tiny[:, T_PF:T_PF + 1], op=ALU.mult)
                            for j in range(NCH):
                                E.tensor_tensor(
                                    out=ch(bigC, j, SL - 1, SL),
                                    in0=ch(bigC, j, SL - 1, SL),
                                    in1=vjnk[:, 8 + j:9 + j], op=ALU.add)
                            SM.ti(E.engine_nop(), "vE", 1)
                        ve_mixed = SM.c["vE"]
                        tapdump("mixed", bigC[:], "vE", ve_mixed)

                        @blk.tensor
                        def _(E):
                            gate(E)
                        if upto == "mixed":
                            finish_dummy()
                            return nc

                with ExitStack() as s5:
                    e5 = s5.enter_context
                    mixo = e5(nc.sbuf_tensor("mixo", [128, NCH * SL], BF16))
                    mwb = e5(nc.sbuf_tensor("mwb", [128, 2 * 2048], BF16))

                    with ExitStack() as pm:
                        psM0 = pm.enter_context(
                            nc.psum_tensor("psM0", [128, 2048], F32))
                        psM1 = pm.enter_context(
                            nc.psum_tensor("psM1", [128, 2048], F32))
                        dg_mx0 = SM.c["dG"]
                        te_mx0 = SM.c["tE"]
                        ve_mx0 = SM.c["vE"]

                        @blk.gpsimd
                        def _(E):
                            SM.w(E, "tE", te_mx0)
                            SM.w(E, "vE", ve_mixed)
                            for t8 in range(NCH):
                                if t8 >= 2:
                                    SM.w(E, "tE", te_mx0 + t8 - 1)
                                SM.ti(E.dma_start(
                                    out=mwb[:, (t8 % 2) * 2048:(t8 % 2) * 2048 + 2048],
                                    in_=bass.AP(mixw_in.tensor, 128 * t8,
                                                [[1024, 128], [131072, 16],
                                                 [1, 128]])), "dG", 16)

                        @blk.tensor
                        def _(E):
                            SM.w(E, "vE", ve_mixed)
                            for t8 in range(NCH):
                                SM.w(E, "dG", dg_mx0 + 16 * (t8 + 1))
                                if t8 >= 2:
                                    SM.w(E, "vE", ve_mx0 + t8 - 1)
                                pm_ = psM0 if t8 % 2 == 0 else psM1
                                for c16 in range(16):
                                    rhs_buf = bigC if c16 < 8 else bigB
                                    cj = c16 if c16 < 8 else c16 - 8
                                    for s4_ in range(4):
                                        _last = E.matmul(
                                            pm_[:, s4_ * 512:(s4_ + 1) * 512],
                                            mwb[:, (t8 % 2) * 2048 + c16 * 128:
                                                (t8 % 2) * 2048 + (c16 + 1) * 128],
                                            ch(rhs_buf, cj, s4_ * 512, (s4_ + 1) * 512),
                                            start=(c16 == 0), stop=(c16 == 15))
                                SM.ti(_last, "tE", 1)

                        @blk.vector
                        def _(E):
                            for t8 in range(NCH):
                                SM.w(E, "tE", te_mx0 + t8 + 1)
                                pm_ = psM0 if t8 % 2 == 0 else psM1
                                SM.ti(E.tensor_copy(out=ch(mixo, t8), in_=pm_[:]),
                                      "vE", 1)
                        ve_mixev = SM.c["vE"]

                        @blk.tensor
                        def _(E):
                            gate(E)

                    with ExitStack() as pn2:
                        psS = pn2.enter_context(
                            nc.psum_tensor("psSo", [128, 2048], F32))
                        psS2 = pn2.enter_context(
                            nc.psum_tensor("psS2o", [128, 2048], F32))
                        se_sq3 = SM.c["sE"]
                        te_ro0 = SM.c["tE"]

                        @blk.scalar
                        def _(E):
                            for j in range(NCH):
                                SM.w(E, "vE", ve_mixev - NCH + j + 1)
                                if j >= 2:
                                    SM.w(E, "tE", te_ro0 + j - 1)
                                SM.ti(E.activation(
                                    out=(bsc0 if j % 2 == 0 else bsc1)[:],
                                    in_=ch(mixo, j), func=AF.Square), "sE", 1)

                        @blk.tensor
                        def _(E):
                            SM.w(E, "vE", ve_mixev)
                            for j in range(NCH):
                                SM.w(E, "sE", se_sq3 + j + 1)
                                for s4_ in range(4):
                                    _last = E.matmul(psS[0:1, s4_ * 512:(s4_ + 1) * 512],
                                             onesbf[:, 0:1],
                                             (bsc0 if j % 2 == 0 else bsc1)
                                             [:, s4_ * 512:(s4_ + 1) * 512],
                                             start=(j == 0), stop=(j == NCH - 1))
                                SM.ti(_last, "tE", 1)
                        te_ro = SM.c["tE"]

                        @blk.vector
                        def _(E):
                            SM.w(E, "tE", te_ro)
                            SM.ti(E.tensor_scalar(
                                out=ROW_TMP(), in0=psS[0:1, :],
                                scalar1=1.0 / D, scalar2=None, op0=ALU.mult),
                                "vE", 1)
                        ve_ro1 = SM.c["vE"]

                        @blk.scalar
                        def _(E):
                            SM.w(E, "vE", ve_ro1)
                            SM.ti(E.activation(out=ROW_BETA(),
                                               in_=ROW_TMP(),
                                               func=AF.Sqrt, bias=tiny[0:1, 70:71]), "sE", 1)
                        se_ro2 = SM.c["sE"]

                        @blk.vector
                        def _(E):
                            SM.w(E, "sE", se_ro2)
                            SM.ti(E.reciprocal(out=ROW_ALPHA(),
                                               in_=ROW_BETA()),
                                  "vE", 1)
                        ve_ro3 = SM.c["vE"]

                        @blk.tensor
                        def _(E):
                            SM.w(E, "vE", ve_ro3)
                            for s4_ in range(4):
                                _last = E.matmul(psS2[:, s4_ * 512:(s4_ + 1) * 512],
                                         ones32[0:1, :],
                                         rowA[0:1,
                                                 s4_ * 512:(s4_ + 1) * 512],
                                         start=True, stop=True)
                            SM.ti(_last, "tE", 1)
                        te_roB = SM.c["tE"]

                        @blk.scalar
                        def _(E):
                            SM.w(E, "tE", te_roB)
                            SM.ti(E.activation(out=alphaB[:], in_=psS2[:],
                                               func=AF.Copy), "sE", 1)
                        se_roB = SM.c["sE"]

                        ve_out0 = SM.c["vE"]

                        @blk.vector
                        def _(E):
                            SM.w(E, "sE", se_roB)
                            for j in range(NCH):
                                E.tensor_tensor(
                                    out=(bsc0 if j % 2 == 0 else bsc1)[:],
                                    in0=ch(mixo, j), in1=alphaB[:], op=ALU.mult)
                                SM.ti(E.scalar_tensor_tensor(
                                    out=ch(mixo, j),
                                    in0=(bsc0 if j % 2 == 0 else bsc1)[:],
                                    scalar=wno_pp[:, j:j + 1], in1=ch(bigA, j),
                                    op0=ALU.mult, op1=ALU.add,
                                    accum_out=busacc[:, j:j + 1]), "vE", 1)
                        ve_out = SM.c["vE"]
                        tapdump("outa", mixo[:], "vE", ve_out)

                        @blk.sync
                        def _(E):
                            SM.w(E, "vE", ve_out)
                            SM.ti(E.dma_start(out=bass.AP(
                                ag_bus_i.tensor, 0, [[8, 128], [1, 8]]),
                                in_=busacc[:]), "dL", 16)
                        dl_bus = SM.c["dL"]

                        @blk.gpsimd
                        def _(E):
                            SM.w(E, "dL", dl_bus)
                            SM.ti(E.collective_compute(
                                "AllGather", ALU.bypass, replica_groups=PAIRS,
                                ins=[ag_bus_i], outs=[ag_bus_o]), "cC", 1)
                        cc_bus = SM.c["cC"]

                        @blk.sync
                        def _(E):
                            SM.w(E, "cC", cc_bus)
                            SM.ti(E.dma_start(out=busacc[:], in_=bass.AP(
                                ag_bus_o.tensor, 0, [[8, 128], [1, 8]])),
                                "dL", 16)
                            SM.ti(E.dma_start(out=emafin[:], in_=bass.AP(
                                ag_bus_o.tensor, 128 * 8, [[8, 128], [1, 8]])),
                                "dL", 16)
                        dl_busr = SM.c["dL"]

                        @blk.vector
                        def _(E):
                            SM.w(E, "dL", dl_busr)
                            vg(E)
                            E.tensor_tensor(out=busacc[:], in0=busacc[:],
                                            in1=emafin[:], op=ALU.add)
                            vg(E)
                            E.tensor_scalar(out=busacc[:], in0=busacc[:],
                                            scalar1=1.0 / float(S), scalar2=None,
                                            op0=ALU.mult)
                            vg(E)
                            SM.ti(E.tensor_copy(out=tinyb[:, 0:8], in_=busacc[:]),
                                  "vE", 1)
                        ve_busm = SM.c["vE"]

                        @blk.tensor
                        def _(E):
                            SM.w(E, "vE", ve_busm)
                            for j in range(NCH):
                                _last = E.matmul(psS2[:, 512:513],
                                         busC[:, j * 128:(j + 1) * 128],
                                         tinyb[:, j:j + 1],
                                         start=(j == 0), stop=(j == NCH - 1))
                            SM.ti(_last, "tE", 1)
                        te_sum = SM.c["tE"]

                        @blk.vector
                        def _(E):
                            SM.w(E, "tE", te_sum)
                            SM.ti(E.tensor_copy(out=tinyb[:, 8:9],
                                                in_=psS2[:, 512:513]), "vE", 1)
                        ve_sum = SM.c["vE"]

                        @blk.tensor
                        def _(E):
                            SM.w(E, "vE", ve_sum)
                            SM.ti(E.matmul(psS2[:, 513:514], busP[:],
                                           tinyb[:, 8:9], start=True, stop=True),
                                  "tE", 1)
                        te_pub = SM.c["tE"]

                        @blk.vector
                        def _(E):
                            SM.w(E, "tE", te_pub)
                            SM.ti(E.tensor_copy(out=tinyb[:, 9:10],
                                                in_=psS2[:, 513:514]), "vE", 1)
                        ve_pub = SM.c["vE"]

                        @blk.tensor
                        def _(E):
                            SM.w(E, "vE", ve_pub)
                            for j in range(NCH):
                                _last = E.matmul(psS2[:, 520 + j:521 + j],
                                         busE[:, j * 128:(j + 1) * 128],
                                         tinyb[:, 9:10], start=True, stop=True)
                            SM.ti(_last, "tE", 1)
                        te_bv = SM.c["tE"]

                        @blk.vector
                        def _(E):
                            SM.w(E, "tE", te_bv)
                            E.tensor_tensor(out=tiny[:, T_BUSV:T_BUSV + 8],
                                            in0=psS2[:, 520:528],
                                            in1=modA[:, 16:24], op=ALU.mult)
                            vg(E)
                            SM.ti(E.tensor_scalar(out=tiny[:, T_BUSV:T_BUSV + 8],
                                                  in0=tiny[:, T_BUSV:T_BUSV + 8],
                                                  scalar1=0.1, scalar2=None,
                                                  op0=ALU.mult), "vE", 1)
                        ve_busv = SM.c["vE"]

                        dl_xr0 = SM.c["dL"]
                        ve_x20 = SM.c["vE"]

                        @blk.sync
                        def _(E):
                            for j in range(NCH):
                                if j >= 2:
                                    SM.w(E, "vE", ve_x20 + j - 1)
                                SM.ti(E.dma_start(
                                    out=(bsc0 if j % 2 == 0 else bsc1)[:],
                                    in_=bass.AP(xbf_scr.tensor, j * 128,
                                                [[D, SL], [1, 128]]),
                                    transpose=True), "dL", 16)

                        @blk.vector
                        def _(E):
                            SM.w(E, "vE", ve_busv)
                            for j in range(NCH):
                                SM.w(E, "dL", dl_xr0 + 16 * (j + 1))
                                E.scalar_tensor_tensor(
                                    out=ch(bigA, j), in0=ch(mixo, j),
                                    scalar=modA[:, 16 + j:17 + j],
                                    in1=(bsc0 if j % 2 == 0 else bsc1)[:],
                                    op0=ALU.mult, op1=ALU.add)
                                SM.ti(E.tensor_scalar(
                                    out=ch(bigA, j), in0=ch(bigA, j),
                                    scalar1=tiny[:, T_BUSV + j:T_BUSV + j + 1],
                                    scalar2=None, op0=ALU.add), "vE", 1)
                        ve_x2 = SM.c["vE"]
                        tapdump("x2", bigA[:], "vE", ve_x2)

                        @blk.sync
                        def _(E):
                            for j in range(NCH):
                                SM.w(E, "vE", ve_x20 + j + 1)
                                SM.ti(E.dma_start(
                                    out=bass.AP(x2_scr.tensor, j * 128 * SL,
                                                [[SL, 128], [1, SL]]),
                                    in_=ch(bigA, j)), "dL", 16)
                        dl_x2s = SM.c["dL"]

                        @blk.tensor
                        def _(E):
                            gate(E)
                        if upto == "x2":
                            finish_dummy()
                            return nc

                with ExitStack() as pn3:
                    psS = pn3.enter_context(nc.psum_tensor("psS3", [128, 2048], F32))
                    psS2 = pn3.enter_context(nc.psum_tensor("psS23", [128, 2048], F32))
                    se_sq4 = SM.c["sE"]
                    te_l20 = SM.c["tE"]

                    @blk.scalar
                    def _(E):
                        SM.w(E, "vE", ve_x2)
                        for j in range(NCH):
                            if j >= 2:
                                SM.w(E, "tE", te_l20 + j - 1)
                            SM.ti(E.activation(
                                out=(bsc0 if j % 2 == 0 else bsc1)[:],
                                in_=ch(bigA, j), func=AF.Square), "sE", 1)

                    @blk.tensor
                    def _(E):
                        for j in range(NCH):
                            SM.w(E, "sE", se_sq4 + j + 1)
                            for s4_ in range(4):
                                _last = E.matmul(psS2[0:1, s4_ * 512:(s4_ + 1) * 512],
                                         onesbf[:, 0:1],
                                         (bsc0 if j % 2 == 0 else bsc1)
                                         [:, s4_ * 512:(s4_ + 1) * 512],
                                         start=(j == 0), stop=(j == NCH - 1))
                            for s4_ in range(4):
                                _last = E.matmul(psS[0:1, s4_ * 512:(s4_ + 1) * 512],
                                         onesbf[:, 0:1],
                                         ch(bigA, j, s4_ * 512, (s4_ + 1) * 512),
                                         start=(j == 0), stop=(j == NCH - 1))
                            SM.ti(_last, "tE", 1)
                    te_l2s = SM.c["tE"]

                    @blk.vector
                    def _(E):
                        SM.w(E, "tE", te_l2s)
                        E.tensor_scalar(out=ROW_MEAN(), in0=psS[0:1, :],
                                        scalar1=1.0 / D, scalar2=None,
                                        op0=ALU.mult)
                        E.tensor_scalar(out=ROW_TMP(), in0=psS2[0:1, :],
                                        scalar1=1.0 / D, scalar2=None,
                                        op0=ALU.mult)
                        E.tensor_tensor(out=ROW_BETA(), in0=ROW_MEAN(),
                                        in1=ROW_MEAN(), op=ALU.mult)
                        SM.ti(E.tensor_tensor(out=ROW_TMP(), in0=ROW_TMP(),
                                              in1=ROW_BETA(),
                                              op=ALU.subtract), "vE", 1)
                    ve_var2 = SM.c["vE"]

                    @blk.scalar
                    def _(E):
                        SM.w(E, "vE", ve_var2)
                        SM.ti(E.activation(out=ROW_BETA(), in_=ROW_TMP(),
                                           func=AF.Sqrt, bias=tiny[0:1, 69:70]), "sE", 1)
                    se_std2 = SM.c["sE"]

                    @blk.vector
                    def _(E):
                        SM.w(E, "sE", se_std2)
                        E.reciprocal(out=ROW_ALPHA(), in_=ROW_BETA())
                        E.tensor_tensor(out=ROW_BETA(), in0=ROW_MEAN(),
                                        in1=ROW_ALPHA(), op=ALU.mult)
                        SM.ti(E.tensor_scalar(out=ROW_BETA(), in0=ROW_BETA(),
                                              scalar1=-1.0, scalar2=None,
                                              op0=ALU.mult), "vE", 1)
                    ve_ab2 = SM.c["vE"]

                    @blk.tensor
                    def _(E):
                        SM.w(E, "vE", ve_ab2)
                        for s4_ in range(4):
                            _last = E.matmul(psS2[:, s4_ * 512:(s4_ + 1) * 512],
                                     ones32[0:1, :],
                                     rowA[0:1,
                                             s4_ * 512:(s4_ + 1) * 512],
                                     start=True, stop=True)
                        SM.ti(_last, "tE", 1)
                    te_aB2 = SM.c["tE"]

                    @blk.scalar
                    def _(E):
                        SM.w(E, "tE", te_aB2)
                        SM.ti(E.activation(out=alphaB[:], in_=psS2[:],
                                           func=AF.Copy), "sE", 1)
                    se_aB2 = SM.c["sE"]

                    @blk.tensor
                    def _(E):
                        SM.w(E, "sE", se_aB2)
                        for s4_ in range(4):
                            _last = E.matmul(psS2[:, s4_ * 512:(s4_ + 1) * 512],
                                     ones32[32:33, :],
                                     rowA[32:33,
                                             s4_ * 512:(s4_ + 1) * 512],
                                     start=True, stop=True)
                        SM.ti(_last, "tE", 1)
                    te_bB2 = SM.c["tE"]

                    @blk.scalar
                    def _(E):
                        SM.w(E, "tE", te_bB2)
                        SM.ti(E.activation(out=betaB[:], in_=psS2[:],
                                           func=AF.Copy), "sE", 1)
                    se_bB2 = SM.c["sE"]

                    ve_h20 = SM.c["vE"]
                    se_h20 = SM.c["sE"]

                    @blk.vector
                    def _(E):
                        SM.w(E, "sE", se_bB2)
                        for j in range(NCH):
                            if j >= 2:
                                SM.w(E, "sE", se_h20 + j - 1)
                            E.tensor_tensor(out=(bsc0 if j % 2 == 0 else bsc1)[:],
                                            in0=ch(bigA, j), in1=alphaB[:],
                                            op=ALU.mult)
                            SM.ti(E.tensor_tensor(
                                out=(bsc0 if j % 2 == 0 else bsc1)[:],
                                in0=(bsc0 if j % 2 == 0 else bsc1)[:],
                                in1=betaB[:], op=ALU.add), "vE", 1)

                    @blk.scalar
                    def _(E):
                        for j in range(NCH):
                            SM.w(E, "vE", ve_h20 + j + 1)
                            SM.ti(E.activation(
                                out=ch(bigB, j),
                                in_=(bsc0 if j % 2 == 0 else bsc1)[:],
                                func=AF.Identity, scale=onepcb[:, j:j + 1],
                                bias=modB[:, j:j + 1]), "sE", 1)
                    se_h2 = SM.c["sE"]
                    tapdump("h2", bigB[:], "sE", se_h2)

                    @blk.vector
                    def _(E):
                        for j in range(NCH):
                            SM.w(E, "sE", se_h2 - NCH + j + 1)
                            E.reduce_max(out=tiny[:, T_SCR + (j % 2):
                                                  T_SCR + (j % 2) + 1],
                                         in_=ch(bigB, j), axis=AX,
                                         apply_absolute_value=True)
                            if j == 1:
                                E.tensor_copy(out=tiny[:, T_SCR + 2:T_SCR + 3],
                                              in_=tiny[:, T_SCR:T_SCR + 1])
                            elif j > 1:
                                E.tensor_tensor(
                                    out=tiny[:, T_SCR + 2:T_SCR + 3],
                                    in0=tiny[:, T_SCR + 2:T_SCR + 3],
                                    in1=tiny[:, T_SCR + ((j - 1) % 2):
                                             T_SCR + ((j - 1) % 2) + 1],
                                    op=ALU.max)
                        vg(E)
                        SM.ti(E.tensor_tensor(
                            out=tiny[:, T_SCR + 1:T_SCR + 2],
                            in0=tiny[:, T_SCR + 2:T_SCR + 3],
                            in1=tiny[:, T_SCR + ((NCH - 1) % 2):
                                     T_SCR + ((NCH - 1) % 2) + 1],
                            op=ALU.max), "vE", 1)
                    ve_h2m = SM.c["vE"]

                    @blk.tensor
                    def _(E):
                        SM.w(E, "vE", ve_h2m)
                        SM.ti(E.transpose(psS2[0:1, 1024:1152],
                                          tiny[:, T_SCR + 1:T_SCR + 2],
                                          ident32[:, 0:128]), "tE", 1)
                    te_h2m = SM.c["tE"]

                    @blk.vector
                    def _(E):
                        SM.w(E, "tE", te_h2m)
                        SM.ti(E.reduce_max(out=tiny[0:1, X_H2L:X_H2L + 1],
                                           in_=psS2[0:1, 1024:1152], axis=AX),
                              "vE", 1)
                    ve_h2mx = SM.c["vE"]

                    @blk.sync
                    def _(E):
                        SM.w(E, "vE", ve_h2mx)
                        SM.ti(E.dma_start(out=ar_h2_i,
                                          in_=tiny[0:1, X_H2L:X_H2L + 1]),
                              "dL", 16)
                    dl_h2m = SM.c["dL"]

                    @blk.gpsimd
                    def _(E):
                        SM.w(E, "dL", dl_h2m)
                        SM.ti(E.collective_compute(
                            "AllReduce", ALU.max, replica_groups=ALL8,
                            ins=[ar_h2_i], outs=[ar_h2_o]), "cC", 1)
                    cc_h2 = SM.c["cC"]

                    @blk.sync
                    def _(E):
                        SM.w(E, "cC", cc_h2)
                        SM.ti(E.dma_start(out=tiny[0:1, X_H2G:X_H2G + 1],
                                          in_=ar_h2_o), "dL", 16)
                    dl_h2r = SM.c["dL"]

                    @blk.vector
                    def _(E):
                        SM.w(E, "dL", dl_h2r)
                        E.tensor_tensor(out=tiny[0:1, X_SX:X_SX + 1],
                                        in0=tiny[0:1, T_SGX:T_SGX + 1],
                                        in1=tiny[0:1, X_H2G:X_H2G + 1], op=ALU.mult)
                        vg(E)
                        E.reciprocal(out=tiny[0:1, X_ISXv:X_ISXv + 1],
                                     in_=tiny[0:1, X_SX:X_SX + 1])
                        E.tensor_scalar(out=tiny[0:1, X_SX:X_SX + 1],
                                        in0=tiny[0:1, X_SX:X_SX + 1],
                                        scalar1=1.0 / 127.0, scalar2=None,
                                        op0=ALU.mult)
                        SM.ti(E.tensor_scalar(out=tiny[0:1, X_ISXv:X_ISXv + 1],
                                              in0=tiny[0:1, X_ISXv:X_ISXv + 1],
                                              scalar1=127.0, scalar2=None,
                                              op0=ALU.mult), "vE", 1)
                    ve_sx = SM.c["vE"]

                    @blk.tensor
                    def _(E):
                        SM.w(E, "vE", ve_sx)
                        _last = E.matmul(psS2[:, 1200:1201], ones32[0:1, 0:128],
                                 tiny[0:1, X_ISXv:X_ISXv + 1], start=True, stop=True)
                        SM.ti(E.matmul(psS2[:, 1201:1202], ones32[0:1, 0:128],
                                       tiny[0:1, X_SX:X_SX + 1],
                                       start=True, stop=True), "tE", 1)
                    te_sxb = SM.c["tE"]

                    @blk.vector
                    def _(E):
                        SM.w(E, "tE", te_sxb)
                        E.tensor_copy(out=tiny[:, T_SX:T_SX + 1],
                                      in_=psS2[:, 1201:1202])
                        E.tensor_copy(out=tiny[:, T_ISX:T_ISX + 1],
                                      in_=psS2[:, 1200:1201])
                        E.tensor_scalar(
                            out=gsc[:],
                            in0=bass.AP(scales_all, 0, [[72, 128], [7, 8], [1, 3]]),
                            scalar1=tiny[:, T_SX:T_SX + 1], scalar2=None,
                            op0=ALU.mult)
                        E.tensor_scalar(
                            out=usc[:],
                            in0=bass.AP(scales_all, 3, [[72, 128], [7, 8], [1, 3]]),
                            scalar1=tiny[:, T_SX:T_SX + 1], scalar2=None,
                            op0=ALU.mult)
                        SM.ti(E.tensor_tensor(
                            out=dsc[:],
                            in0=bass.AP(scales_all, 6, [[72, 128], [7, 8]]),
                            in1=modB[:, 16:24], op=ALU.mult), "vE", 1)
                    ve_ffnsc = SM.c["vE"]
                    tapdump("qsc", tiny[:], "vE", ve_ffnsc)
                    tapdump("gsc", gsc[:], "vE", ve_ffnsc)

                    se_xq0 = SM.c["sE"]
                    ve_xq0 = SM.c["vE"]

                    @blk.scalar
                    def _(E):
                        SM.w(E, "vE", ve_ffnsc)
                        for j in range(NCH):
                            if j >= 2:
                                SM.w(E, "vE", ve_xq0 + j - 1)
                            SM.ti(E.activation(
                                out=(fsc0 if j % 2 == 0 else fsc1)[:],
                                in_=ch(bigB, j), func=AF.Identity,
                                scale=tiny[:, T_ISX:T_ISX + 1], bias=tiny[:, 68:69]),
                                "sE", 1)

                    @blk.vector
                    def _(E):
                        for j in range(NCH):
                            SM.w(E, "sE", se_xq0 + j + 1)
                            E.tensor_scalar(out=(fsc0 if j % 2 == 0 else fsc1)[:],
                                            in0=(fsc0 if j % 2 == 0 else fsc1)[:],
                                            scalar1=CMAG, scalar2=127.0,
                                            op0=ALU.subtract, op1=ALU.min)
                            SM.ti(E.tensor_scalar(
                                out=ch(bigC, j),
                                in0=(fsc0 if j % 2 == 0 else fsc1)[:],
                                scalar1=-128.0, scalar2=None, op0=ALU.max),
                                "vE", 1)
                    ve_xq = SM.c["vE"]
                    tapdump("xq", bigC[:], "vE", ve_xq)

                    @blk.tensor
                    def _(E):
                        gate(E)
                    if upto == "xq":
                        finish_dummy()
                        return nc

            with ExitStack() as s6:
                e6b = s6.enter_context
                wgT = e6b(nc.sbuf_tensor("wgT", [128, NCH * IF], BF16))
                wuT = e6b(nc.sbuf_tensor("wuT", [128, NCH * IF], BF16))

                dl_wt0 = SM.c["dL"]

                @blk.sync
                def _(E):
                    gate(E)
                    SM.w(E, "cC", cc_wall)
                    for j in range(NCH):
                        SM.ti(E.dma_start(
                            out=wgT[:, j * IF:(j + 1) * IF],
                            in_=bass.AP(ag_wg_o.tensor, j * 128, [[D, IF], [1, 128]]),
                            transpose=True), "dL", 16)
                    for j in range(NCH):
                        SM.ti(E.dma_start(
                            out=wuT[:, j * IF:(j + 1) * IF],
                            in_=bass.AP(ag_wu_o.tensor, j * 128, [[D, IF], [1, 128]]),
                            transpose=True), "dL", 16)
                dl_wt = SM.c["dL"]

                with ExitStack() as pg:
                    psG = pg.enter_context(nc.psum_tensor("psG", [128, 2048], F32))
                    psU = pg.enter_context(nc.psum_tensor("psU", [128, 2048], F32))

                    te_f0 = SM.c["tE"]
                    se_f0 = SM.c["sE"]
                    ve_f0 = SM.c["vE"]
                    dl_f0 = SM.c["dL"]


                    @blk.tensor
                    def _(E):
                        SM.w(E, "dL", dl_wt)
                        SM.w(E, "vE", ve_xq)
                        for it in range(NIT):
                            for sh in range(2):
                                idx = 2 * it + sh
                                if idx >= 2:
                                    SM.w(E, "vE", ve_f0 + idx - 1)
                                for j in range(NCH):
                                    for s2 in range(2):
                                        _last = E.matmul(
                                            psG[:, sh * 1024 + s2 * 512:
                                                sh * 1024 + (s2 + 1) * 512],
                                            wgT[:, j * IF + it * 128:
                                                j * IF + (it + 1) * 128],
                                            ch(bigC, j, sh * 1024 + s2 * 512,
                                               sh * 1024 + (s2 + 1) * 512),
                                            start=(j == 0), stop=(j == NCH - 1))
                                for j in range(NCH):
                                    for s2 in range(2):
                                        _last = E.matmul(
                                            psU[:, sh * 1024 + s2 * 512:
                                                sh * 1024 + (s2 + 1) * 512],
                                            wuT[:, j * IF + it * 128:
                                                j * IF + (it + 1) * 128],
                                            ch(bigC, j, sh * 1024 + s2 * 512,
                                               sh * 1024 + (s2 + 1) * 512),
                                            start=(j == 0), stop=(j == NCH - 1))
                                SM.ti(_last, "tE", 1)

                    @blk.scalar
                    def _(E):
                        for it in range(NIT):
                            for sh in range(2):
                                idx = 2 * it + sh
                                SM.w(E, "tE", te_f0 + idx + 1)
                                if idx >= 2:
                                    SM.w(E, "vE", ve_f0 + idx - 1)
                                SM.ti(E.activation(
                                    out=bsc0[:, sh * 1024:(sh + 1) * 1024],
                                    in_=psG[:, sh * 1024:(sh + 1) * 1024],
                                    func=AF.Silu, scale=gsc[:, it:it + 1]),
                                    "sE", 1)

                    @blk.vector
                    def _(E):
                        for it in range(NIT):
                            for sh in range(2):
                                idx = 2 * it + sh
                                SM.w(E, "sE", se_f0 + idx + 1)
                                if idx >= 2:
                                    SM.w(E, "dL", dl_f0 + 16 * (idx - 1))
                                SM.ti(E.scalar_tensor_tensor(
                                    out=bsc1[:, sh * 1024:(sh + 1) * 1024],
                                    in0=psU[:, sh * 1024:(sh + 1) * 1024],
                                    scalar=usc[:, it:it + 1],
                                    in1=bsc0[:, sh * 1024:(sh + 1) * 1024],
                                    op0=ALU.mult, op1=ALU.mult), "vE", 1)
                                pc = T_SCR + 19 + (idx % 2)
                                qc = T_SCR + 19 + ((idx - 1) % 2)
                                E.reduce_max(out=tiny[:, pc:pc + 1],
                                             in_=bsc1[:, sh * 1024:(sh + 1) * 1024],
                                             axis=AX, apply_absolute_value=True)
                                if idx == 1:
                                    E.tensor_copy(out=tiny[:, T_SCR + 22:
                                                           T_SCR + 23],
                                                  in_=tiny[:, qc:qc + 1])
                                elif idx >= 2:
                                    E.tensor_tensor(
                                        out=tiny[:, T_SCR + 22:T_SCR + 23],
                                        in0=tiny[:, T_SCR + 22:T_SCR + 23],
                                        in1=tiny[:, qc:qc + 1], op=ALU.max)
                        vg(E)
                        lc = T_SCR + 19 + ((2 * NIT - 1) % 2)
                        SM.ti(E.tensor_tensor(
                            out=tiny[:, T_SCR + 21:T_SCR + 22],
                            in0=tiny[:, T_SCR + 22:T_SCR + 23],
                            in1=tiny[:, lc:lc + 1], op=ALU.max), "vE", 1)

                    @blk.sync
                    def _(E):
                        for it in range(NIT):
                            for sh in range(2):
                                idx = 2 * it + sh
                                SM.w(E, "vE", ve_f0 + idx + 1)
                                SM.ti(E.dma_start(
                                    out=bass.AP(hpre_scr.tensor,
                                                it * 128 * SL + sh * 1024,
                                                [[SL, 128], [1, 1024]]),
                                    in_=bsc1[:, sh * 1024:(sh + 1) * 1024]),
                                    "dL", 16)
                    dl_hpre = SM.c["dL"]

                    ve_hpm = SM.c["vE"]

                    @blk.tensor
                    def _(E):
                        gate(E)

                with ExitStack() as ph:
                    psH = ph.enter_context(nc.psum_tensor("psH", [128, 512], F32))

                    @blk.tensor
                    def _(E):
                        SM.w(E, "vE", ve_hpm)
                        SM.ti(E.transpose(psH[0:1, 0:128],
                                          tiny[:, T_SCR + 21:T_SCR + 22],
                                          ident32[:, 0:128]), "tE", 1)
                    te_hm = SM.c["tE"]

                    @blk.vector
                    def _(E):
                        SM.w(E, "tE", te_hm)
                        SM.ti(E.reduce_max(out=tiny[0:1, X_HPL:X_HPL + 1],
                                           in_=psH[0:1, 0:128], axis=AX), "vE", 1)
                    ve_hm = SM.c["vE"]

                    @blk.sync
                    def _(E):
                        SM.w(E, "vE", ve_hm)
                        SM.ti(E.dma_start(out=ar_hp_i,
                                          in_=tiny[0:1, X_HPL:X_HPL + 1]),
                              "dL", 16)
                    dl_hm = SM.c["dL"]

                    @blk.gpsimd
                    def _(E):
                        SM.w(E, "dL", dl_hm)
                        SM.ti(E.collective_compute(
                            "AllReduce", ALU.max, replica_groups=ALL8,
                            ins=[ar_hp_i], outs=[ar_hp_o]), "cC", 1)
                    cc_hp = SM.c["cC"]

                    @blk.sync
                    def _(E):
                        SM.w(E, "cC", cc_hp)
                        SM.ti(E.dma_start(out=tiny[0:1, X_HPG:X_HPG + 1],
                                          in_=ar_hp_o), "dL", 16)
                    dl_hmr = SM.c["dL"]

                    @blk.vector
                    def _(E):
                        SM.w(E, "dL", dl_hmr)
                        E.tensor_tensor(out=tiny[0:1, X_SH:X_SH + 1],
                                        in0=tiny[0:1, T_SGD:T_SGD + 1],
                                        in1=tiny[0:1, X_HPG:X_HPG + 1], op=ALU.mult)
                        vg(E)
                        E.reciprocal(out=tiny[0:1, X_ISHv:X_ISHv + 1],
                                     in_=tiny[0:1, X_SH:X_SH + 1])
                        E.tensor_scalar(out=tiny[0:1, X_SH:X_SH + 1],
                                        in0=tiny[0:1, X_SH:X_SH + 1],
                                        scalar1=1.0 / 127.0, scalar2=None,
                                        op0=ALU.mult)
                        SM.ti(E.tensor_scalar(out=tiny[0:1, X_ISHv:X_ISHv + 1],
                                              in0=tiny[0:1, X_ISHv:X_ISHv + 1],
                                              scalar1=127.0, scalar2=None,
                                              op0=ALU.mult), "vE", 1)
                    ve_sh = SM.c["vE"]

                    @blk.tensor
                    def _(E):
                        SM.w(E, "vE", ve_sh)
                        _last = E.matmul(psH[:, 128:129], ones32[0:1, 0:128],
                                 tiny[0:1, X_ISHv:X_ISHv + 1], start=True, stop=True)
                        SM.ti(E.matmul(psH[:, 129:130], ones32[0:1, 0:128],
                                       tiny[0:1, X_SH:X_SH + 1],
                                       start=True, stop=True), "tE", 1)
                    te_shb = SM.c["tE"]

                    @blk.vector
                    def _(E):
                        SM.w(E, "tE", te_shb)
                        E.tensor_copy(out=tiny[:, T_SCR:T_SCR + 1],
                                      in_=psH[:, 129:130])
                        E.tensor_copy(out=tiny[:, T_ISH:T_ISH + 1],
                                      in_=psH[:, 128:129])
                        SM.ti(E.tensor_scalar(out=dscf[:], in0=dsc[:],
                                              scalar1=tiny[:, T_SCR:T_SCR + 1],
                                              scalar2=None, op0=ALU.mult), "vE", 1)
                    ve_dscf = SM.c["vE"]

                    @blk.tensor
                    def _(E):
                        gate(E)
                    if upto == "gateup":
                        finish_dummy()
                        return nc

        with ExitStack() as s7:
            e7 = s7.enter_context
            hq = e7(nc.sbuf_tensor("hq", [128, NIT * 1024], BF16))
            wdT = e7(nc.sbuf_tensor("wdT", [128, NIT * 1024], BF16))
            ofin = e7(nc.sbuf_tensor("ofin", [128, NCH * 1024], F32))
            x2b = e7(nc.sbuf_tensor("x2b", [128, 2 * 1024], BF16))
            obuf = e7(nc.sbuf_tensor("obuf", [128, 2 * 1024], F32))

            dl_wd0 = SM.c["dL"]

            @blk.sync
            def _(E):
                gate(E)
                for i in range(NIT):
                    SM.ti(E.dma_start(
                        out=wdT[:, i * 1024:(i + 1) * 1024],
                        in_=bass.AP(ag_wd_o.tensor, i * 128, [[IF, D], [1, 128]]),
                        transpose=True), "dL", 16)
            dl_wd = SM.c["dL"]

            with ExitStack() as pd:
                psD0 = pd.enter_context(nc.psum_tensor("psD0", [128, 1024], F32))
                psD1 = pd.enter_context(nc.psum_tensor("psD1", [128, 1024], F32))
                psT0 = pd.enter_context(nc.psum_tensor("psT0", [128, 1024], F32))
                psT1 = pd.enter_context(nc.psum_tensor("psT1", [128, 1024], F32))

                for sh in range(2):
                    dl_q0 = SM.c["dL"]
                    se_q0 = SM.c["sE"]
                    ve_q0 = SM.c["vE"]

                    @blk.sync
                    def _(E):
                        for it in range(NIT):
                            if it >= 2:
                                SM.w(E, "sE", se_q0 + it - 1)
                            elif sh == 1:
                                SM.w(E, "sE", se_q0)
                            SM.ti(E.dma_start(
                                out=(bsc0 if it % 2 == 0 else bsc1)[:, 0:1024],
                                in_=bass.AP(hpre_scr.tensor,
                                            it * 128 * SL + sh * 1024,
                                            [[SL, 128], [1, 1024]])), "dL", 16)

                    @blk.scalar
                    def _(E):
                        if sh == 0:
                            SM.w(E, "vE", ve_dscf)
                        for it in range(NIT):
                            SM.w(E, "dL", dl_q0 + 16 * (it + 1))
                            if it >= 2:
                                SM.w(E, "vE", ve_q0 + 2 * (it - 2) + 2)
                            elif sh == 1:
                                SM.w(E, "vE", ve_q0)
                            SM.ti(E.activation(
                                out=(fsc0 if it % 2 == 0 else fsc1)[:, 0:1024],
                                in_=(bsc0 if it % 2 == 0 else bsc1)[:, 0:1024],
                                func=AF.Identity,
                                scale=tiny[:, T_ISH:T_ISH + 1], bias=tiny[:, 68:69]),
                                "sE", 1)

                    @blk.vector
                    def _(E):
                        if sh == 0:
                            SM.w(E, "vE", ve_dscf)
                        for it in range(NIT):
                            SM.w(E, "sE", se_q0 + it + 1)
                            SM.ti(E.tensor_scalar(
                                out=(fsc0 if it % 2 == 0 else fsc1)[:, 0:1024],
                                in0=(fsc0 if it % 2 == 0 else fsc1)[:, 0:1024],
                                scalar1=CMAG, scalar2=127.0,
                                op0=ALU.subtract, op1=ALU.min), "vE", 1)
                            SM.ti(E.tensor_scalar(
                                out=hq[:, it * 1024:(it + 1) * 1024],
                                in0=(fsc0 if it % 2 == 0 else fsc1)[:, 0:1024],
                                scalar1=-128.0, scalar2=None, op0=ALU.max),
                                "vE", 1)
                    ve_hq = SM.c["vE"]

                    dl_x2r0 = SM.c["dL"]

                    @blk.sync
                    def _(E):
                        for t in range(NCH):
                            if t >= 2:
                                SM.w(E, "vE", ve_hq + t - 1)
                            SM.ti(E.dma_start(
                                out=x2b[:, (t % 2) * 1024:(t % 2) * 1024 + 1024],
                                in_=bass.AP(x2_scr.tensor, t * 128 * SL + sh * 1024,
                                            [[SL, 128], [1, 1024]])), "dL", 16)
                    dl_x2r = SM.c["dL"]

                    te_d0 = SM.c["tE"]
                    ve_d0 = SM.c["vE"]

                    @blk.tensor
                    def _(E):
                        SM.w(E, "vE", ve_hq)
                        SM.w(E, "dL", dl_wd)
                        for t in range(NCH):
                            if t >= 2:
                                SM.w(E, "vE", ve_d0 + t - 1)
                            pd_ = psD0 if t % 2 == 0 else psD1
                            for i in range(NIT):
                                for s2 in range(2):
                                    _last = E.matmul(
                                        pd_[:, s2 * 512:(s2 + 1) * 512],
                                        wdT[:, i * 1024 + t * 128:
                                            i * 1024 + (t + 1) * 128],
                                        hq[:, i * 1024 + s2 * 512:
                                           i * 1024 + (s2 + 1) * 512],
                                        start=(i == 0), stop=(i == NIT - 1))
                            SM.ti(_last, "tE", 1)

                    @blk.vector
                    def _(E):
                        for t in range(NCH):
                            SM.w(E, "tE", te_d0 + t + 1)
                            SM.w(E, "dL", dl_x2r0 + 16 * (t + 1))
                            pd_ = psD0 if t % 2 == 0 else psD1
                            SM.ti(E.scalar_tensor_tensor(
                                out=ofin[:, t * 1024:(t + 1) * 1024],
                                in0=pd_[:], scalar=dscf[:, t:t + 1],
                                in1=x2b[:, (t % 2) * 1024:(t % 2) * 1024 + 1024],
                                op0=ALU.mult, op1=ALU.add), "vE", 1)
                    ve_dfin = SM.c["vE"]

                    te_o0 = SM.c["tE"]
                    ve_o0 = SM.c["vE"]

                    @blk.tensor
                    def _(E):
                        SM.w(E, "vE", ve_dfin)
                        for stile in range(8):
                            if stile >= 2:
                                SM.w(E, "vE", ve_o0 + stile - 1)
                            pt_ = psT0 if stile % 2 == 0 else psT1
                            for t in range(NCH):
                                _last = E.transpose(pt_[:, t * 128:(t + 1) * 128],
                                            ofin[:, t * 1024 + stile * 128:
                                                 t * 1024 + (stile + 1) * 128],
                                            ident32[:])
                                E.nop()
                            SM.ti(_last, "tE", 1)

                    dl_o0 = SM.c["dL"]

                    @blk.vector
                    def _(E):
                        for stile in range(8):
                            SM.w(E, "tE", te_o0 + stile + 1)
                            if stile >= 2:
                                SM.w(E, "dL", dl_o0 + 16 * (stile - 1))
                            pt_ = psT0 if stile % 2 == 0 else psT1
                            SM.ti(E.tensor_copy(
                                out=obuf[:, (stile % 2) * 1024:(stile % 2) * 1024 + 1024],
                                in_=pt_[:]), "vE", 1)

                    @blk.sync
                    def _(E):
                        for stile in range(8):
                            SM.w(E, "vE", ve_o0 + stile + 1)
                            SM.ti(E.dma_start(
                                out=bass.AP(y_out.tensor,
                                            (sh * 1024 + stile * 128) * D,
                                            [[D, 128], [1, D]]),
                                in_=obuf[:, (stile % 2) * 1024:(stile % 2) * 1024 + 1024]),
                                "dL", 16)

                @blk.sync
                def _(E):
                    gate(E)

                @blk.gpsimd
                def _(E):
                    gate(E)

                @blk.tensor
                def _(E):
                    gate(E)

                @blk.vector
                def _(E):
                    gate(E)

                @blk.scalar
                def _(E):
                    gate(E)
    return nc


_IDENT = np.eye(128, dtype=np.float32)
_ONES = np.ones((128, 128), dtype=np.float32)
_GRP4 = np.zeros((128, 4), dtype=np.float32)
for _p in range(128):
    _GRP4[_p, _p // 32] = 1.0


def _prep_inputs(inputs):
    gi = {k: np.ascontiguousarray(np.asarray(v)) for k, v in inputs.items()}
    wqk_cat = np.concatenate([gi["L0_attn_slr_wq1"], gi["L0_attn_slr_wq2"],
                              gi["L0_attn_slr_wk1"], gi["L0_attn_slr_wk2"]],
                             axis=1)
    wg_pad = np.zeros((8 * WSL, D), np.float32)
    wg_pad[:IF] = gi["L0_ffn_w_gate"]
    wu_pad = np.zeros((8 * WSL, D), np.float32)
    wu_pad[:IF] = gi["L0_ffn_w_up"]
    x = gi["L0_attn_x"]

    in_maps = []
    for c in range(8):
        b, hf = c // 2, c % 2
        m = {
            "x": np.ascontiguousarray(x[b, hf * SL:(hf + 1) * SL, :]),
            "c_row": gi["L0_c"][b:b + 1, :],
            "wmod_sl": np.ascontiguousarray(
                gi["L0_wmod"][:, hf * 3072:(hf + 1) * 3072]),
            "wn_in": gi["L0_attn_w_norm_in"].reshape(8, 128),
            "wn_out": gi["L0_attn_w_norm_out"].reshape(8, 128),
            "rw1": gi["L0_attn_router_w1"],
            "rw2": gi["L0_attn_router_w2"],
            "rb2": gi["L0_attn_router_b2"].reshape(1, 3),
            "lam_in": np.tile(gi["L0_attn_lambdas"][None, :], (128, 1)),
            "wqk": wqk_cat,
            "wv_in": gi["L0_attn_slr_wv"],
            "wo_in": gi["L0_attn_slr_wo"],
            "slam_in": np.full((128, 1), np.asarray(gi["L0_attn_slr_lam"]).reshape(-1)[0],
                               np.float32),
            "lm_in": gi["L0_attn_landmarks"],
            "mixw_in": gi["L0_attn_mix_w"],
            "busc_in": gi["L0_attn_bus_compress"],
            "buse_in": gi["L0_attn_bus_expand"],
            "busp_in": gi["L0_attn_bus_publish"],
            "clipx_in": gi["L0_ffn_clip_x"].reshape(1, 1),
            "clipd_in": gi["L0_ffn_clip_down"].reshape(1, 1),
            "wg_sl": np.ascontiguousarray(wg_pad[c * WSL:(c + 1) * WSL]),
            "wu_sl": np.ascontiguousarray(wu_pad[c * WSL:(c + 1) * WSL]),
            "wd_sl": np.ascontiguousarray(
                gi["L0_ffn_w_down"][c * 128:(c + 1) * 128]),
            "flag1": np.full((128, 1), float(hf), np.float32),
            "ident32_in": _IDENT,
            "identbf_in": _IDENT,
            "ones32_in": _ONES,
            "grp4_in": _GRP4,
        }
        in_maps.append(m)
    return in_maps


def kernel(**inputs):
    in_maps = _prep_inputs(inputs)
    nc = build_nc()
    res = run_bass_kernel_spmd(nc, in_maps, core_ids=list(range(8)))
    out = np.empty((B, S, D), np.float32)
    for c in range(8):
        b, hf = c // 2, c % 2
        out[b, hf * SL:(hf + 1) * SL, :] = np.asarray(res.results[c]["y"],
                                                      np.float32)
    return out
